# version 13
# baseline (speedup 1.0000x reference)
import numpy as np
from collections import deque
from contextlib import ExitStack

import concourse.bass as bass
import concourse.mybir as mybir
from concourse.bass_utils import run_bass_kernel_spmd
from concourse.aot_env import allow_ici_all_to_all

F32 = mybir.dt.float32
BF16 = mybir.dt.bfloat16
AF = mybir.ActivationFunctionType
ALU = mybir.AluOpType
AXX = mybir.AxisListType.X

NCORES = 8
L = 16384
DM = 1024
DI = 2048
DS = 16
DC = DI // NCORES
TC = L // NCORES
LC1 = 1024
NCH1 = L // LC1
LC2 = 2048
NCH2 = L // LC2
NT = (DC * DS) // 128
SL = 256
EPS = 1e-5
BUDGET = 100


class _Plan:
    ENGINES = ("sync", "gpsimd", "tensor", "vector", "scalar")
    AUTO = {"vector": "V", "scalar": "A", "gpsimd": "G"}

    def __init__(self, nc):
        self.nc = nc
        self.ops = {e: [] for e in self.ENGINES}
        self.sem_h = {}
        self.sem_v = {}
        self.pend = deque()

    def make_sems(self, ctx, names):
        for n in names:
            self.sem_h[n] = ctx.enter_context(self.nc.semaphore(n))
            self.sem_v[n] = 0

    def inc(self, eng, sem, fn, n=1):
        if eng in self.AUTO and sem == self.AUTO[eng]:
            return self._auto(eng, fn)
        h = self.sem_h[sem]
        self.sem_v[sem] += n
        v = self.sem_v[sem]

        def _op(e, fn=fn, h=h, n=n):
            fn(e).then_inc(h, n)
        self.ops[eng].append(_op)
        return v

    def _auto(self, eng, fn):
        sem = self.AUTO[eng]
        h = self.sem_h[sem]
        prev = self.sem_v[sem]
        if prev > 0:
            def _w(e, h=h, prev=prev):
                e.wait_ge(h, prev)
            self.ops[eng].append(_w)
        self.sem_v[sem] = prev + 1

        def _op(e, fn=fn, h=h):
            fn(e).then_inc(h, 1)
        self.ops[eng].append(_op)
        return prev + 1

    def run(self, eng, fn):
        if eng in self.AUTO:
            self._auto(eng, fn)
            return
        self.ops[eng].append(fn)

    def wait(self, eng, sem, v):
        if v <= 0:
            return
        h = self.sem_h[sem]

        def _op(e, h=h, v=v):
            e.wait_ge(h, v)
        self.ops[eng].append(_op)

    def val(self, sem):
        return self.sem_v[sem]

    def q_mm(self, waits, fn):
        self.sem_v["F"] += 1
        v = self.sem_v["F"]
        self.pend.append((list(waits), fn, v))
        return v

    def q_drain(self, k):
        h = self.sem_h["F"]
        n = 0
        while self.pend and n < k:
            waits, fn, v = self.pend.popleft()
            for (sem, val) in waits:
                self.wait("tensor", sem, val)

            def _op(e, fn=fn, h=h):
                fn(e).then_inc(h, 1)
            self.ops["tensor"].append(_op)
            n += 1


def build_program(taps=()):
    nc = bass.Bass("TRN2", target_bir_lowering=False, debug=False)

    def din(name, shape, dtype=F32):
        return nc.dram_tensor(name, shape, dtype, kind="ExternalInput").ap()

    def dout(name, shape, dtype=F32):
        return nc.dram_tensor(name, shape, dtype, kind="ExternalOutput").ap()

    def dint(name, shape, dtype=F32):
        return nc.dram_tensor(name, shape, dtype).ap()

    xT = din("xT", [DM, L])
    x_res = din("x_res", [TC, DM])
    winc = din("winc", [DM, 2 * DC])
    xpw_pat = din("xpw_pat", [128, 2 * 33])
    dtw_sm = din("dtw_sm", [128, 2])
    dtb_sm = din("dtb_sm", [128, 2])
    alog_pat = din("alog_pat", [128, NT])
    d_pat = din("d_pat", [128, 2])
    convw_pat = din("convw_pat", [128, 8])
    convb_pat = din("convb_pat", [128, 2])
    g_exp = din("g_exp", [128, 16 * 128])
    ident_in = din("ident_in", [128, 128])
    ones_in = din("ones_in", [1, 128])
    wout_in = din("wout_in", [DI, DM])
    wr_pat = din("wr_pat", [128, 64])
    br_in = din("br_in", [1, 8])
    b1_my = din("b1_my", [128, 16])
    b2_my = din("b2_my", [1, DM])
    w1c = din("w1c", [DM, DI])
    w2c = din("w2c", [DI, DM])

    out = dout("out", [TC, DM])

    xs_d = dint("xs_d", [DC, L], BF16)
    zs_d = dint("zs_d", [DC, L], BF16)
    u_d = dint("u_d", [DC, L], BF16)
    dt_d = dint("dt_d", [DC, L], BF16)
    x1_d = dint("x1_d", [TC, DM])
    woutb_d = dint("woutb_d", [DI, DM], BF16)
    br_i = [dint(f"br_i{c}", [33, LC2]) for c in range(NCH2)]
    br_o = [dint(f"br_o{c}", [33, LC2]) for c in range(NCH2)]
    a2_i = [dint(f"a2_i{c}", [NCORES, DC, SL], BF16) for c in range(NCH2)]
    a2_o = [dint(f"a2_o{c}", [NCORES, DC, SL], BF16) for c in range(NCH2)]
    ag_i = [dint(f"ag_i{c}", [DM, SL], BF16) for c in range(NCH2)]
    ag_o = [dint(f"ag_o{c}", [NCORES, DM, SL], BF16) for c in range(NCH2)]
    rs_i = [dint(f"rs_i{c}", [LC2, DM], BF16) for c in range(NCH2)]
    rs_o = [dint(f"rs_o{c}", [SL, DM], BF16) for c in range(NCH2)]

    tap_outs = {}
    for t in taps:
        shp, dt = {
            "xs": ([DC, L], BF16), "zs": ([DC, L], BF16), "u": ([DC, L], BF16),
            "x1": ([TC, DM], F32),
        }[t]
        tap_outs[t] = dout("tap_" + t, shp, dt)

    with ExitStack() as octx:
        def sbuf(ctx, name, shape, dtype):
            return ctx.enter_context(nc.sbuf_tensor(name, shape, dtype))

        pb = [octx.enter_context(nc.psum_tensor(f"pb{i}", [128, 512], F32))
              for i in range(8)]

        c_dtws = sbuf(octx, "c_dtws", [128, 2], F32)
        c_dtbs = sbuf(octx, "c_dtbs", [128, 2], F32)
        c_A = sbuf(octx, "c_A", [128, NT], F32)
        c_D = sbuf(octx, "c_D", [128, 2], F32)
        c_cw = sbuf(octx, "c_cw", [128, 8], F32)
        c_cb = sbuf(octx, "c_cb", [128, 2], F32)
        c_id = sbuf(octx, "c_id", [128, 128], F32)
        c_ones = sbuf(octx, "c_ones", [1, 128], BF16)
        c_wr = sbuf(octx, "c_wr", [128, 64], BF16)
        c_br = sbuf(octx, "c_br", [1, 8], BF16)
        c_b1 = sbuf(octx, "c_b1", [128, 16], F32)
        c_b2 = sbuf(octx, "c_b2", [1, DM], BF16)
        c_xpw = sbuf(octx, "c_xpw", [128, 2 * 33], BF16)
        c_g = sbuf(octx, "c_g", [128, 16 * 128], BF16)
        c_k = sbuf(octx, "c_k", [128, 3], F32)
        carry = sbuf(octx, "carry", [128, NT], F32)
        w1e = sbuf(octx, "w1e", [128, 8 * 2048], BF16)
        w2e = sbuf(octx, "w2e", [128, 16 * 1024], BF16)

        P = _Plan(nc)
        P.make_sems(octx, [
            "boot", "boot2", "xt0", "xt1", "sp1", "pre", "bmc", "brs",
            "ur0", "ur1", "ys0", "ys1", "y1t", "xres0", "xres1",
            "x1s0", "x1s1", "wol", "wo2", "wl3", "wl4", "shp", "tp",
            "wsc0", "wsc1", "xfl", "xf2", "smc", "rst", "os0", "os1",
            "T", "F", "V", "A", "G", "CCR", "CCX", "CCG", "CCS"])

        for dst, src in [(c_dtws, dtw_sm), (c_dtbs, dtb_sm), (c_D, d_pat),
                         (c_cw, convw_pat), (c_cb, convb_pat),
                         (c_id, ident_in), (c_b1, b1_my)]:
            P.inc("sync", "boot", lambda e, d=dst, s=src: e.dma_start(out=d[:, :], in_=s), 16)
        P.inc("sync", "boot", lambda e: e.dma_start(out=c_A[:, :], in_=alog_pat), 16)
        v_boot = P.val("boot")
        for dst, src in [(c_ones, ones_in), (c_wr, wr_pat), (c_br, br_in),
                         (c_b2, b2_my), (c_xpw, xpw_pat), (c_g, g_exp)]:
            P.inc("gpsimd", "boot2", lambda e, d=dst, s=src: e.dma_start(out=d[:, :], in_=s), 16)
        for kt in range(8):
            P.inc("gpsimd", "boot2", lambda e, kt=kt: e.dma_start(
                out=w1e[:, kt * 2048:(kt + 1) * 2048],
                in_=w1c[kt * 128:(kt + 1) * 128, :]), 16)
        for ft in range(16):
            P.inc("gpsimd", "boot2", lambda e, ft=ft: e.dma_start(
                out=w2e[:, ft * 1024:(ft + 1) * 1024],
                in_=w2c[ft * 128:(ft + 1) * 128, :]), 16)
        v_boot2 = P.val("boot2")

        P.wait("vector", "boot", v_boot)
        P.run("vector", lambda e: e.memset(c_k[:, 0:1], EPS))
        P.run("vector", lambda e: e.memset(c_k[:, 1:2], -0.5))
        v_ck = P.inc("vector", "V", lambda e: e.memset(c_k[:, 2:3], -1.0))
        P.wait("scalar", "boot", v_boot)
        P.run("scalar", lambda e: e.activation(c_A[:, :], c_A[:, :], AF.Exp))
        P.wait("scalar", "V", v_ck)
        v_cA = P.inc("scalar", "A", lambda e: e.mul(c_A[:, :], c_A[:, :], c_k[:, 2:3]))

        with ExitStack() as sA1:
            c_win = sbuf(sA1, "c_win", [128, 8 * 512], BF16)
            xTb = [sbuf(sA1, f"xTb{i}", [128, 8 * LC1], BF16) for i in range(2)]
            x_buf = [sbuf(sA1, f"x_buf{i}", [128, 3 + LC1], F32) for i in range(2)]
            cv = [sbuf(sA1, f"cv{i}", [128, LC1], F32) for i in range(2)]
            xs_c = [sbuf(sA1, f"xs_c{i}", [128, LC1], BF16) for i in range(2)]
            zs_c = [sbuf(sA1, f"zs_c{i}", [128, LC1], BF16) for i in range(2)]
            bcdp = sbuf(sA1, "bcdp", [33, L], F32)
            wsc = [sbuf(sA1, f"wsc{i}", [128, 1024], BF16) for i in range(2)]

            P.wait("gpsimd", "boot2", v_boot2)
            for kt in range(8):
                P.inc("gpsimd", "boot2", lambda e, kt=kt: e.dma_start(
                    out=c_win[:, kt * 512:(kt + 1) * 512],
                    in_=winc[kt * 128:(kt + 1) * 128, :]), 16)
            v_win = P.val("boot2")

            P.run("vector", lambda e: e.memset(x_buf[0][:, 0:3], 0.0))
            P.inc("vector", "V", lambda e: e.memset(x_buf[1][:, 0:3], 0.0))

            v_wo_st = [0, 0]
            wo_i = 0
            v_T_ip = {}
            v_A_ev = {}
            v_V_conv = {}
            v_A_xs = {}
            v_T_xp = {}
            v_V_bc = {}
            v_sp = {}
            v_ld = {}
            v_ar = {}

            v_wsc = {0: 0, 1: 0}
            for ch in range(NCH1):
                cb = ch % 2
                if ch >= 2:
                    P.wait("gpsimd", "T", v_T_ip[(ch - 2, 3)])
                    P.wait("gpsimd", f"xt{cb}", v_ld[ch - 2])
                for kt in range(8):
                    P.inc("gpsimd", f"xt{cb}",
                          lambda e, kt=kt, ch=ch, cb=cb: e.dma_start(
                              out=xTb[cb][:, kt * LC1:(kt + 1) * LC1],
                              in_=xT[kt * 128:(kt + 1) * 128,
                                     ch * LC1:(ch + 1) * LC1]), 16)
                v_ld[ch] = P.val(f"xt{cb}")
                for _ in range(2):
                    if wo_i < 16:
                        wb = wo_i % 2
                        wsem = "wol" if wb == 0 else "wo2"
                        if v_wo_st[wb]:
                            P.wait("gpsimd", wsem, v_wo_st[wb])
                        P.wait("gpsimd", f"wsc{wb}", v_wsc[wb])
                        vg = P.inc("gpsimd", f"wsc{wb}",
                                   lambda e, wb=wb, wo_i=wo_i: e.dma_start(
                                       out=wsc[wb][:, :],
                                       in_=wout_in[wo_i * 128:(wo_i + 1) * 128, :]), 16)
                        v_wsc[wb] = vg
                        P.wait("sync", f"wsc{wb}", vg)
                        v_wo_st[wb] = P.inc(
                            "sync", wsem,
                            lambda e, wb=wb, wo_i=wo_i: e.dma_start(
                                out=woutb_d[wo_i * 128:(wo_i + 1) * 128, :],
                                in_=wsc[wb][:, :]), 16)
                        wo_i += 1
                v_wo_done = (v_wo_st[0], v_wo_st[1])

                if ch < 2:
                    P.wait("tensor", "boot2", v_win)
                P.wait("tensor", f"xt{cb}", v_ld[ch])
                for ct in range(4):
                    pp = (pb[0], pb[1]) if ct % 2 == 0 else (pb[2], pb[3])
                    if (ch, ct) not in ((0, 0), (0, 1)):
                        prev = (ch, ct - 2) if ct >= 2 else (ch - 1, ct + 2)
                        P.wait("tensor", "A", v_A_ev[prev])
                    for kt in range(8):
                        for nh in range(2):
                            P.inc("tensor", "T",
                                  lambda e, kt=kt, ct=ct, cb=cb, pq=pp[nh], nh=nh:
                                  e.matmul(
                                      pq[:, :],
                                      c_win[:, kt * 512 + ct * 128:
                                            kt * 512 + (ct + 1) * 128],
                                      xTb[cb][:, kt * LC1 + nh * 512:
                                              kt * LC1 + (nh + 1) * 512],
                                      start=(kt == 0), stop=(kt == 7)))
                    v_T_ip[(ch, ct)] = P.val("T")

                    P.wait("scalar", "T", v_T_ip[(ch, ct)])
                    if ct < 2:
                        if ch >= 1:
                            P.wait("scalar", "V", v_V_conv[(ch - 1, ct)])
                        for nh in range(2):
                            v_A_ev[(ch, ct)] = P.inc(
                                "scalar", "A",
                                lambda e, ct=ct, nh=nh, pq=pp[nh]: e.copy(
                                    x_buf[ct][:, 3 + nh * 512:3 + (nh + 1) * 512],
                                    pq[:, :]))
                    else:
                        pt = ct - 2
                        if ch >= 1:
                            P.wait("scalar", "sp1", v_sp[ch - 1])
                        for nh in range(2):
                            v_A_ev[(ch, ct)] = P.inc(
                                "scalar", "A",
                                lambda e, pt=pt, nh=nh, pq=pp[nh]: e.activation(
                                    zs_c[pt][:, nh * 512:(nh + 1) * 512],
                                    pq[:, :], AF.Silu))
                for pt in range(2):
                    P.wait("vector", "A", v_A_ev[(ch, pt)])
                    P.run("vector", lambda e, pt=pt: e.tensor_scalar(
                        cv[pt][:, :], x_buf[pt][:, 0:LC1],
                        c_cw[:, pt * 4:pt * 4 + 1], None, ALU.mult))
                    for k in (1, 2, 3):
                        P.run("vector", lambda e, pt=pt, k=k: e.scalar_tensor_tensor(
                            cv[pt][:, :], x_buf[pt][:, k:k + LC1],
                            c_cw[:, pt * 4 + k:pt * 4 + k + 1],
                            cv[pt][:, :], ALU.mult, ALU.add))
                    v_V_conv[(ch, pt)] = P.inc(
                        "vector", "V",
                        lambda e, pt=pt: e.tensor_copy(
                            x_buf[pt][:, 0:3], x_buf[pt][:, LC1:LC1 + 3]))
                    P.wait("scalar", "V", v_V_conv[(ch, pt)])
                    if ch >= 1:
                        P.wait("scalar", "sp1", v_sp[ch - 1])
                    v_A_xs[(ch, pt)] = P.inc(
                        "scalar", "A",
                        lambda e, pt=pt: e.activation(
                            xs_c[pt][:, :], cv[pt][:, :], AF.Silu,
                            bias=c_cb[:, pt:pt + 1]))
                P.wait("sync", "A", max(v_A_xs[(ch, 1)], v_A_ev[(ch, 3)]))
                if ch >= 1:
                    P.wait("sync", "sp1", v_sp[ch - 1])
                for pt in range(2):
                    P.inc("sync", "sp1", lambda e, pt=pt, ch=ch: e.dma_start(
                        out=xs_d[pt * 128:(pt + 1) * 128,
                                 ch * LC1:(ch + 1) * LC1],
                        in_=xs_c[pt][:, :]), 16)
                    P.inc("sync", "sp1", lambda e, pt=pt, ch=ch: e.dma_start(
                        out=zs_d[pt * 128:(pt + 1) * 128,
                                 ch * LC1:(ch + 1) * LC1],
                        in_=zs_c[pt][:, :]), 16)
                v_sp[ch] = P.val("sp1")

                P.wait("tensor", "A", v_A_xs[(ch, 1)])
                if ch >= 1:
                    P.wait("tensor", "V", v_V_bc[ch - 1])
                for kt in range(2):
                    for nh in range(2):
                        P.inc("tensor", "T", lambda e, kt=kt, nh=nh: e.matmul(
                            pb[4 + nh][0:33, :],
                            c_xpw[:, kt * 33:(kt + 1) * 33],
                            xs_c[kt][:, nh * 512:(nh + 1) * 512],
                            start=(kt == 0), stop=(kt == 1)))
                v_T_xp[ch] = P.val("T")
                P.wait("vector", "T", v_T_xp[ch])
                for nh in range(2):
                    v_V_bc[ch] = P.inc(
                        "vector", "V", lambda e, ch=ch, nh=nh: e.tensor_copy(
                            bcdp[:, ch * LC1 + nh * 512:
                                 ch * LC1 + (nh + 1) * 512],
                            pb[4 + nh][0:33, :]))

                if ch % 2 == 1:
                    c2 = ch // 2
                    P.wait("sync", "V", v_V_bc[ch])
                    P.wait("sync", "brs", 16 * c2)
                    v_sp1 = P.inc("sync", "brs",
                                  lambda e, c2=c2: e.dma_start(
                                      out=br_i[c2],
                                      in_=bcdp[:, c2 * LC2:(c2 + 1) * LC2]), 16)
                    P.wait("gpsimd", "brs", v_sp1)
                    P.wait("gpsimd", "sp1", v_sp[ch])
                    v_ar[c2] = P.inc("gpsimd", "CCR",
                                     lambda e, c2=c2: e.collective_compute(
                                         "AllReduce", ALU.add,
                                         replica_groups=[list(range(NCORES))],
                                         ins=[br_i[c2].opt()],
                                         outs=[br_o[c2].opt()]))

            if "xs" in taps:
                P.wait("sync", "sp1", P.val("sp1"))
                P.inc("sync", "tp", lambda e: e.dma_start(
                    out=tap_outs["xs"], in_=xs_d), 16)
            if "zs" in taps:
                P.wait("sync", "sp1", P.val("sp1"))
                P.inc("sync", "tp", lambda e: e.dma_start(
                    out=tap_outs["zs"], in_=zs_d), 16)

            v_ph1 = {s: P.val(s) for s in ("T", "V", "A")}

        with ExitStack() as sA2:
            bm_rep = sbuf(sA2, "bm_rep", [128, LC2], BF16)
            cc_rep = sbuf(sA2, "cc_rep", [128, LC2], BF16)
            xs2 = [sbuf(sA2, f"xs2_{i}", [128, LC2], BF16) for i in range(2)]
            zs2 = [sbuf(sA2, f"zs2_{i}", [128, LC2], BF16) for i in range(2)]
            dt2 = [sbuf(sA2, f"dt2_{i}", [128, LC2], BF16) for i in range(2)]
            w_rep = [sbuf(sA2, f"w_rep{i}", [128, LC2], BF16) for i in range(2)]
            dA2 = [sbuf(sA2, f"dA2_{i}", [128, LC2], F32) for i in range(2)]
            u_rep = [sbuf(sA2, f"u_rep{i}", [128, LC2], BF16) for i in range(2)]
            dbx = [sbuf(sA2, f"dbx{i}", [128, LC2], BF16) for i in range(2)]
            h2 = sbuf(sA2, "h2", [128, LC2], BF16)
            y1T = sbuf(sA2, "y1T", [128, 16 * SL], BF16)
            wot = [sbuf(sA2, f"wot{i}", [128, 1024], BF16) for i in range(2)]
            xr = [sbuf(sA2, f"xr{i}", [128, 1024], F32) for i in range(2)]
            sq3 = sbuf(sA2, "sq3", [128, 512], F32)
            x1T = sbuf(sA2, "x1T", [128, 8 * SL], BF16)
            xfT = sbuf(sA2, "xfT", [128, 8 * 1024], BF16)
            hb3 = sbuf(sA2, "hb3", [128, 16 * SL], BF16)
            rsb = [sbuf(sA2, f"rsb{i}", [128, 1024], BF16) for i in range(2)]
            rs2 = [sbuf(sA2, f"rs2_{i}", [128, 1024], BF16) for i in range(2)]
            gates = sbuf(sA2, "gates", [128, 16 * 8], F32)
            gl2 = sbuf(sA2, "gl2", [128, 64], F32)
            rtc = sbuf(sA2, "rtc", [128, 40], F32)
            lnc = sbuf(sA2, "lnc", [128, 48], F32)

            for eng in _Plan.ENGINES:
                for s in ("T", "V", "A"):
                    P.wait(eng, s, v_ph1[s])
            P.wait("tensor", "boot2", v_boot2)
            P.wait("scalar", "A", v_cA)

            v_A_tile = {}
            v_dbx = {}
            v_hc = {}
            v_scan = {}
            v_T_tile = {}
            v_V_y1 = {}
            v_sld_rep = {}
            v_y1g = {}
            vg1 = {}
            st3 = {}
            p3q = deque()

            def pop_steps(budget=BUDGET):
                n = 0
                while p3q and n < budget:
                    fcnt, fn = p3q.popleft()
                    fn()
                    n += max(fcnt, 1)

            def mk_y1t(c):
                def _s(c=c):
                    P.wait("gpsimd", "CCX", st3[("a2a", c)])
                    P.wait("gpsimd", "y1t", P.val("y1t"))
                    if c >= 1:
                        P.wait("gpsimd", "F", st3[("opjF", c - 1)])
                    for kt in range(16):
                        P.inc("gpsimd", "y1t", lambda e, kt=kt, c=c: e.dma_start(
                            out=y1T[:, kt * SL:(kt + 1) * SL],
                            in_=a2_o[c][kt // 2,
                                        (kt % 2) * 128:(kt % 2 + 1) * 128, :]), 16)
                    st3[("y1tl", c)] = P.val("y1t")
                return _s

            def mk_opj(c, kt):
                def _s(c=c, kt=kt):
                    wb = kt % 2
                    wsem = "wl3" if wb == 0 else "wl4"
                    if st3.get(("wotmm", wb), 0):
                        P.wait("gpsimd", "F", st3[("wotmm", wb)])
                    P.wait("gpsimd", "wol" if wb == 0 else "wo2",
                           v_wo_done[wb])
                    P.wait("gpsimd", wsem, st3.get(("wolv", wb), 0))
                    P.inc("gpsimd", wsem, lambda e, kt=kt, wb=wb: e.dma_start(
                        out=wot[wb][:, :],
                        in_=woutb_d[kt * 128:(kt + 1) * 128, :]), 16)
                    v_wl = P.val(wsem)
                    st3[("wolv", wb)] = v_wl
                    wts = [("y1t", st3[("y1tl", c)]), (wsem, v_wl)]
                    if kt == 0 and c >= 1:
                        wts.append(("V", st3.get(("rtrV",), 0)))
                        wts.append(("A", st3.get(("rse_last",), 0)))
                    v = 0
                    for lt in range(2):
                        for nh in range(2):
                            v = P.q_mm(wts,
                                       lambda e, kt=kt, lt=lt, nh=nh, wb=wb,
                                       pq=pb[4 + lt * 2 + nh]:
                                       e.matmul(
                                           pq[:, :],
                                           y1T[:, kt * SL + lt * 128:
                                               kt * SL + (lt + 1) * 128],
                                           wot[wb][:, nh * 512:(nh + 1) * 512],
                                           start=(kt == 0), stop=(kt == 15)))
                            wts = []
                        st3[("opjf", c, lt)] = v
                    st3[("wotmm", wb)] = v
                    if kt == 15:
                        st3[("opjF", c)] = v
                return _s

            def _ln_math(o, buf):
                P.run("vector", lambda e: e.reduce_sum(
                    lnc[:, o:o + 1], buf[:, :], axis=AXX))
                P.run("vector", lambda e: e.tensor_tensor(
                    sq3[:, :], buf[:, 0:512], buf[:, 0:512], ALU.mult))
                P.run("vector", lambda e: e.reduce_sum(
                    lnc[:, o + 1:o + 2], sq3[:, :], axis=AXX))
                P.run("vector", lambda e: e.tensor_tensor(
                    sq3[:, :], buf[:, 512:1024], buf[:, 512:1024], ALU.mult))
                P.run("vector", lambda e: e.reduce_sum(
                    lnc[:, o + 10:o + 11], sq3[:, :], axis=AXX))
                P.run("vector", lambda e: e.tensor_tensor(
                    lnc[:, o + 1:o + 2], lnc[:, o + 1:o + 2],
                    lnc[:, o + 10:o + 11], ALU.add))
                P.run("vector", lambda e: e.tensor_scalar(
                    lnc[:, o + 2:o + 3], lnc[:, o:o + 1], 1.0 / DM, None, ALU.mult))
                P.run("vector", lambda e: e.tensor_scalar(
                    lnc[:, o + 3:o + 4], lnc[:, o + 1:o + 2], 1.0 / DM, None, ALU.mult))
                P.run("vector", lambda e: e.tensor_tensor(
                    lnc[:, o + 4:o + 5], lnc[:, o + 2:o + 3],
                    lnc[:, o + 2:o + 3], ALU.mult))
                P.run("vector", lambda e: e.tensor_tensor(
                    lnc[:, o + 5:o + 6], lnc[:, o + 3:o + 4],
                    lnc[:, o + 4:o + 5], ALU.subtract))
                v_Vvar = P.inc("vector", "V", lambda e: e.tensor_scalar(
                    lnc[:, o + 5:o + 6], lnc[:, o + 5:o + 6], EPS, None, ALU.add))
                P.wait("scalar", "V", v_Vvar)
                P.run("scalar", lambda e: e.activation(
                    lnc[:, o + 6:o + 7], lnc[:, o + 5:o + 6], AF.Ln))
                v_Ai = P.inc("scalar", "A", lambda e: e.activation(
                    lnc[:, o + 7:o + 8], lnc[:, o + 6:o + 7],
                    AF.Exp, scale=c_k[:, 1:2]))
                P.wait("vector", "A", v_Ai)
                P.run("vector", lambda e: e.tensor_tensor(
                    lnc[:, o + 8:o + 9], lnc[:, o + 2:o + 3],
                    lnc[:, o + 7:o + 8], ALU.mult))
                v_Vnb = P.inc("vector", "V", lambda e: e.tensor_scalar(
                    lnc[:, o + 9:o + 10], lnc[:, o + 8:o + 9],
                    c_k[:, 2:3], None, ALU.mult))
                P.wait("scalar", "V", v_Vnb)
                v_n = P.inc("scalar", "A", lambda e: e.activation(
                    buf[:, :], buf[:, :], AF.Identity,
                    bias=lnc[:, o + 9:o + 10], scale=lnc[:, o + 7:o + 8]))
                return v_n

            def mk_ln1(c, lt):
                def _s(c=c, lt=lt):
                    lb = lt
                    o = lt * 12
                    tok0 = c * SL + lt * 128
                    P.wait("scalar", f"os{lb}", st3.get(("ost", c - 2, lt), 0))
                    P.wait("scalar", f"x1s{lb}", st3.get(("x1s", c - 1, lt), 0))
                    P.wait("scalar", "F", st3.get(("trF", c - 1), 0))
                    P.wait("scalar", f"xres{lb}", P.val(f"xres{lb}"))
                    P.inc("scalar", f"xres{lb}",
                          lambda e, tok0=tok0, lb=lb: e.dma_start(
                              out=xr[lb][:, :],
                              in_=x_res[tok0:tok0 + 128, :]), 16)
                    v_xr = P.val(f"xres{lb}")
                    P.wait("vector", "F", st3[("opjf", c, lt)])
                    P.wait("vector", f"xres{lb}", v_xr)
                    P.wait("vector", "A", st3.get(("x1v", c - 1, lt), 0))
                    v_add = 0
                    for nh in range(2):
                        v_add = P.inc("vector", "V",
                                      lambda e, lb=lb, nh=nh,
                                      pq=pb[4 + lt * 2 + nh]:
                                      e.tensor_tensor(
                                          xr[lb][:, nh * 512:(nh + 1) * 512],
                                          pq[:, :],
                                          xr[lb][:, nh * 512:(nh + 1) * 512],
                                          ALU.add))
                    st3[("add", c, lt)] = v_add
                    v_x1 = _ln_math(o, xr[lb])
                    st3[("x1v", c, lt)] = v_x1
                    P.wait("scalar", "A", v_x1)
                    P.wait("scalar", f"x1s{lb}", P.val(f"x1s{lb}"))
                    P.inc("scalar", f"x1s{lb}",
                          lambda e, tok0=tok0, lb=lb: e.dma_start(
                              out=x1_d[tok0:tok0 + 128, :], in_=xr[lb][:, :]), 16)
                    st3[("x1s", c, lt)] = P.val(f"x1s{lb}")
                return _s

            def mk_tr(c, lt):
                def _s(c=c, lt=lt):
                    wts = [("A", st3[("x1v", c, lt)]),
                           ("V", st3[("add", c, lt)])]
                    v = 0
                    for j in range(8):
                        v = P.q_mm(wts, lambda e, j=j, lt=lt,
                                   pq=pb[4 + lt * 2 + j // 4]: e.transpose(
                                       pq[:, (j % 4) * 128:(j % 4 + 1) * 128],
                                       xr[lt][:, j * 128:(j + 1) * 128],
                                       c_id[:, :]))
                        wts = []
                    st3[("trF", c)] = v
                    P.wait("scalar", "F", v)
                    if c >= 1:
                        P.wait("scalar", "shp", st3[("agship", c - 1)])
                    v_te = 0
                    for half in range(2):
                        v_te = P.inc("scalar", "A",
                                     lambda e, lt=lt, half=half,
                                     pq=pb[4 + lt * 2 + half]: e.copy(
                                         x1T[:, :]
                                         .rearrange("p (j s) -> p j s", j=8)
                                         [:, half * 4:(half + 1) * 4,
                                          lt * 128:(lt + 1) * 128],
                                         pq[:, :].rearrange(
                                             "p (j f) -> p j f", j=4)))
                    st3[("te", c, lt)] = v_te
                return _s

            def mk_ship(c):
                def _s(c=c):
                    P.wait("scalar", "A",
                           max(st3[("te", c, 0)], st3[("te", c, 1)]))
                    P.wait("scalar", "shp", P.val("shp"))
                    P.inc("scalar", "shp", lambda e, c=c: e.dma_start(
                        out=ag_i[c].rearrange("(j p) s -> p j s", j=8),
                        in_=x1T[:, :].rearrange("p (j s) -> p j s", j=8)), 16)
                    st3[("agship", c)] = P.val("shp")
                    P.wait("gpsimd", "shp", st3[("agship", c)])
                    st3[("ag", c)] = P.inc(
                        "gpsimd", "CCG", lambda e, c=c: e.collective_compute(
                            "AllGather", ALU.bypass,
                            replica_groups=[list(range(NCORES))],
                            ins=[ag_i[c].opt()], outs=[ag_o[c].opt()]))
                return _s

            def mk_ln2(c, gk, lt):
                def _s(c=c, gk=gk, lt=lt):
                    lb = lt
                    o = 24 + lt * 12
                    tok0 = c * SL + lt * 128
                    P.wait("scalar", "CCS", st3[("rs", c)])
                    P.wait("scalar", "F", st3[("trF", gk)])
                    P.wait("scalar", f"x1s{lb}", st3[("x1s", gk, lt)])
                    if c >= 1:
                        P.wait("scalar", f"os{lb}", st3[("ost", c - 1, lt)])
                    P.wait("scalar", f"xres{lb}", P.val(f"xres{lb}"))
                    P.inc("scalar", f"xres{lb}", lambda e, c=c, lt=lt, lb=lb:
                          e.dma_start(out=rs2[lb][:, :],
                                      in_=rs_o[c][lt * 128:(lt + 1) * 128, :]), 16)
                    P.inc("scalar", f"xres{lb}", lambda e, tok0=tok0, lb=lb:
                          e.dma_start(out=xr[lb][:, :],
                                      in_=x1_d[tok0:tok0 + 128, :]), 16)
                    v_l2 = P.val(f"xres{lb}")
                    P.wait("vector", f"xres{lb}", v_l2)
                    P.wait("vector", "A", st3.get(("ln2v", c - 1, lt), 0))
                    P.run("vector", lambda e, lb=lb: e.tensor_tensor(
                        xr[lb][:, :], xr[lb][:, :], rs2[lb][:, :], ALU.add))
                    v_o = _ln_math(o, xr[lb])
                    st3[("ln2v", c, lt)] = v_o
                    P.wait("scalar", "A", v_o)
                    P.inc("scalar", f"os{lb}", lambda e, tok0=tok0, lb=lb:
                          e.dma_start(out=out[tok0:tok0 + 128, :],
                                      in_=xr[lb][:, :]), 16)
                    st3[("ost", c, lt)] = P.val(f"os{lb}")
                return _s

            def mk_xflq(c, hf, q):
                def _s(c=c, hf=hf, q=q):
                    sem = "xfl" if q % 2 == 0 else "xf2"
                    if q == 0:
                        P.wait("gpsimd", "CCG", st3[("ag", c)])
                    P.wait("gpsimd", "F", st3.get(("m1l", q), 0))
                    P.wait("gpsimd", sem, P.val(sem))
                    for kt in range(8):
                        P.inc("gpsimd", sem,
                              lambda e, kt=kt, c=c, hf=hf, q=q: e.dma_start(
                                  out=xfT[:, kt * 1024 + q * 256:
                                          kt * 1024 + (q + 1) * 256],
                                  in_=ag_o[c][4 * hf + q,
                                              kt * 128:(kt + 1) * 128, :]), 16)
                    st3[("xflv", q)] = (sem, P.val(sem))
                return _s

            def mk_rtr(c, hf, lt, gc):
                def _s(c=c, hf=hf, lt=lt, gc=gc):
                    wts = [st3[("xflv", lt // 2)]]
                    if lt == 0:
                        wts.append(("boot2", v_boot2))
                        wts.append(("A", st3.get(("te", gc, 0), 0)))
                        wts.append(("V", st3.get(("rtrV",), 0)))
                    ppo = lt * 8
                    for kt in range(8):
                        P.q_mm(wts, lambda e, kt=kt, ppo=ppo, lt=lt: e.matmul(
                            pb[4][:, ppo:ppo + 8],
                            xfT[:, kt * 1024 + lt * 128:
                                kt * 1024 + (lt + 1) * 128],
                            c_wr[:, kt * 8:(kt + 1) * 8],
                            start=(kt == 0), stop=False))
                        wts = []
                    st3[("rfv",)] = P.q_mm([], lambda e, ppo=ppo: e.matmul(
                        pb[4][:, ppo:ppo + 8], c_ones[:1, :], c_br[:1, :],
                        start=False, stop=True))
                return _s

            def mk_gates(c, hf):
                def _s(c=c, hf=hf):
                    r3 = lambda ap: ap.rearrange("p (l e) -> p l e", e=8)
                    bc = lambda col: rtc[:, col:col + 8].unsqueeze(2)\
                        .broadcast_to([128, 8, 8])
                    go = hf * 64
                    P.wait("vector", "F", st3[("rfv",)])
                    P.wait("vector", "A", st3.get(("rse_last",), 0))
                    v_cp = P.inc("vector", "V", lambda e: e.tensor_copy(
                        sq3[:, 0:64], pb[4][:, 0:64]))
                    st3[("rtrV",)] = v_cp
                    P.run("vector", lambda e: e.reduce_max(
                        rtc[:, 0:8].unsqueeze(2), r3(sq3[:, 0:64]), axis=AXX))
                    P.run("vector", lambda e, go=go: e.tensor_tensor(
                        r3(gates[:, go:go + 64]), r3(sq3[:, 0:64]), bc(0),
                        ALU.is_equal))
                    P.run("vector", lambda e, go=go: e.scalar_tensor_tensor(
                        gl2[:, 0:64], gates[:, go:go + 64], -1e30, sq3[:, 0:64],
                        ALU.mult, ALU.add))
                    P.run("vector", lambda e: e.reduce_max(
                        rtc[:, 8:16].unsqueeze(2), r3(gl2[:, 0:64]), axis=AXX))
                    v_Vd = P.inc("vector", "V", lambda e: e.tensor_tensor(
                        rtc[:, 16:24], rtc[:, 8:16], rtc[:, 0:8], ALU.subtract))
                    P.wait("scalar", "V", v_Vd)
                    v_Ae = P.inc("scalar", "A", lambda e: e.activation(
                        rtc[:, 24:32], rtc[:, 16:24], AF.Exp))
                    P.wait("vector", "A", v_Ae)
                    P.run("vector", lambda e: e.tensor_scalar(
                        rtc[:, 32:40], rtc[:, 24:32], 1.0, None, ALU.add))
                    P.run("vector", lambda e: e.reciprocal(
                        rtc[:, 32:40], rtc[:, 32:40]))
                    P.run("vector", lambda e: e.tensor_tensor(
                        rtc[:, 24:32], rtc[:, 24:32], rtc[:, 32:40], ALU.mult))
                    P.run("vector", lambda e, go=go: e.tensor_tensor(
                        r3(gates[:, go:go + 64]), r3(gates[:, go:go + 64]),
                        bc(32), ALU.mult))
                    P.run("vector", lambda e: e.tensor_tensor(
                        r3(gl2[:, 0:64]), r3(gl2[:, 0:64]), bc(8), ALU.is_equal))
                    P.run("vector", lambda e: e.tensor_tensor(
                        r3(gl2[:, 0:64]), r3(gl2[:, 0:64]), bc(24), ALU.mult))
                    v_gv = P.inc("vector", "V", lambda e, go=go: e.tensor_tensor(
                        gates[:, go:go + 64], gl2[:, 0:64], gates[:, go:go + 64],
                        ALU.add))
                    st3[("gv", c, hf)] = v_gv
                return _s

            def mk_mm1(c, hf, q, ft, gc):
                def _s(c=c, hf=hf, q=q, ft=ft, gc=gc):
                    fb = ft % 2
                    q0 = q * 256
                    wts = []
                    if ft == 0:
                        wts.append(st3[("xflv", q)])
                    if q == 0 and ft <= 1:
                        if ft == 0:
                            wts.append(("V", st3.get(("rtrV",), 0)))
                        wts.append(("A", max(st3.get(("te", gc, 0), 0),
                                             st3.get(("h3A", fb), 0))))
                    else:
                        prevh = st3.get(("h3A", fb), 0)
                        if prevh:
                            wts.append(("A", prevh))
                    v = 0
                    for kt in range(8):
                        v = P.q_mm(wts, lambda e, kt=kt, ft=ft, q0=q0,
                                   pq=pb[4 + fb]:
                                   e.matmul(
                                       pq[:, 0:256],
                                       w1e[:, kt * 2048 + ft * 128:
                                           kt * 2048 + (ft + 1) * 128],
                                       xfT[:, kt * 1024 + q0:
                                           kt * 1024 + q0 + 256],
                                       start=(kt == 0), stop=(kt == 7)))
                        wts = []
                    P.wait("scalar", "F", v)
                    v_h3 = P.inc("scalar", "A", lambda e, ft=ft, pq=pb[4 + fb]:
                                 e.activation(
                                     hb3[:, ft * SL:(ft + 1) * SL],
                                     pq[:, 0:256],
                                     AF.Relu, bias=c_b1[:, ft:ft + 1]))
                    st3[("h3A", fb)] = v_h3
                    st3[("h3", ft)] = v_h3
                    if ft == 15:
                        st3[("m1l", q)] = v
                return _s

            def mk_mm2(c, hf, q, tt, gc):
                def _s(c=c, hf=hf, q=q, tt=tt, gc=gc):
                    g0 = st3.get(("rsePD",), 0)
                    if q == 0 and tt == 0:
                        g0 = max(g0, st3.get(("te", gc, 1), 0))
                    v = 0
                    for ft in range(16):
                        wts = [("A", max(st3[("h3", ft)], g0 if ft == 0 else 0))]
                        for nh in range(2):
                            v = P.q_mm(wts,
                                       lambda e, ft=ft, tt=tt, nh=nh,
                                       pq=pb[6 + nh]: e.matmul(
                                           pq[:, :],
                                           hb3[:, ft * SL + tt * 128:
                                               ft * SL + (tt + 1) * 128],
                                           w2e[:, ft * 1024 + nh * 512:
                                               ft * 1024 + (nh + 1) * 512],
                                           start=(ft == 0), stop=False))
                            wts = []
                    for nh in range(2):
                        v = P.q_mm([], lambda e, nh=nh, pq=pb[6 + nh]: e.matmul(
                            pq[:, :], c_ones[:1, :],
                            c_b2[:1, nh * 512:(nh + 1) * 512],
                            start=False, stop=True))
                    glt = hf * 8 + q * 2 + tt
                    P.wait("scalar", "F", v)
                    P.wait("scalar", "V", st3[("gv", c, hf)])
                    P.wait("scalar", "rst", st3.get(("rstv", tt), 0))
                    v_rse = 0
                    for nh in range(2):
                        v_rse = P.inc("scalar", "A",
                                      lambda e, tt=tt, glt=glt, nh=nh,
                                      pq=pb[6 + nh]:
                                      e.activation(
                                          rsb[tt][:, nh * 512:(nh + 1) * 512],
                                          pq[:, :], AF.Identity,
                                          scale=gates[:, glt * 8:glt * 8 + 1]))
                    st3[("rsePD",)] = v_rse
                    st3[("rse_last",)] = v_rse
                    P.wait("gpsimd", "A", v_rse)
                    P.wait("gpsimd", "rst", P.val("rst"))
                    tok = hf * 1024 + q * 256 + tt * 128
                    P.inc("gpsimd", "rst", lambda e, c=c, tok=tok, tt=tt:
                          e.dma_start(out=rs_i[c][tok:tok + 128, :],
                                      in_=rsb[tt][:, :]), 16)
                    st3[("rstv", tt)] = P.val("rst")
                return _s

            def mk_rs(c):
                def _s(c=c):
                    P.wait("gpsimd", "rst", P.val("rst"))
                    st3[("rs", c)] = P.inc(
                        "gpsimd", "CCS", lambda e, c=c: e.collective_compute(
                            "ReduceScatter", ALU.add,
                            replica_groups=[list(range(NCORES))],
                            ins=[rs_i[c].opt()], outs=[rs_o[c].opt()]))
                return _s

            def push_p3a(c):
                A = p3q.append
                A((1, mk_y1t(c)))
                for kt in range(16):
                    A((4, mk_opj(c, kt)))
                for lt in range(2):
                    A((1, mk_ln1(c, lt)))
                for lt in range(2):
                    A((8, mk_tr(c, lt)))
                A((1, mk_ship(c)))

            def push_p3b(c, hf):
                A = p3q.append
                gc = c if hf == 0 else c + 1
                for q in range(4):
                    A((2, mk_xflq(c, hf, q)))
                for lt in range(8):
                    A((9, mk_rtr(c, hf, lt, gc)))
                A((2, mk_gates(c, hf)))
                for q in range(4):
                    for ft in range(16):
                        A((8, mk_mm1(c, hf, q, ft, gc)))
                    for tt in range(2):
                        A((34, mk_mm2(c, hf, q, tt, gc)))
                    if hf == 0 and q == 1 and c >= 1:
                        for lt in range(2):
                            A((1, mk_ln2(c - 1, c, lt)))
                if hf == 1:
                    A((1, mk_rs(c)))

            v_u_sp = 0
            v_srep = 0
            for ch in range(NCH2):
                lo = ch * LC2
                P.wait("sync", "CCR", v_ar[ch])
                if ch >= 1:
                    P.wait("sync", "ys0", v_y1g[(ch - 1, 0)])
                    P.wait("sync", "ys1", v_y1g[(ch - 1, 1)])
                    P.wait("sync", "pre", v_u_sp)
                for pt in range(2):
                    P.inc("sync", "pre", lambda e, pt=pt, lo=lo: e.dma_start(
                        out=xs2[pt][:, :],
                        in_=xs_d[pt * 128:(pt + 1) * 128, lo:lo + LC2]), 16)
                    P.inc("sync", "pre", lambda e, pt=pt, lo=lo: e.dma_start(
                        out=zs2[pt][:, :],
                        in_=zs_d[pt * 128:(pt + 1) * 128, lo:lo + LC2]), 16)
                P.wait("gpsimd", "CCR", v_ar[ch])
                if ch >= 1:
                    P.wait("gpsimd", "V", v_hc[(ch - 1, NT - 1)])
                    P.wait("gpsimd", "bmc", vg1[ch - 1])
                    P.wait("gpsimd", "smc", v_srep)
                P.inc("gpsimd", "smc", lambda e, ch=ch: e.dma_start(
                    out=h2[:, :],
                    in_=br_o[ch][32:33, :]
                    .unsqueeze(0).broadcast_to([128, 1, LC2])), 16)
                v_srep = P.val("smc")
                P.inc("gpsimd", "bmc", lambda e, ch=ch: e.dma_start(
                    out=bm_rep[:, :],
                    in_=br_o[ch][0:16, :]
                    .unsqueeze(1).broadcast_to([16, 8, LC2])), 16)
                vg1[ch] = P.inc("gpsimd", "bmc", lambda e, ch=ch: e.dma_start(
                    out=cc_rep[:, :],
                    in_=br_o[ch][16:32, :]
                    .unsqueeze(1).broadcast_to([16, 8, LC2])), 16)
                v_pre = P.val("pre")

                P.wait("scalar", "pre", v_pre)
                P.wait("scalar", "smc", v_srep)
                if ch >= 1:
                    P.wait("scalar", "V", v_V_y1[(ch - 1, 1)])
                P.run("scalar", lambda e: e.activation(
                    dt2[0][:, :], h2[:, :], AF.Exp,
                    bias=c_dtbs[:, 0:1], scale=c_dtws[:, 0:1]))
                vAx = P.inc("scalar", "A", lambda e: e.activation(
                    dt2[1][:, :], h2[:, :], AF.Exp,
                    bias=c_dtbs[:, 1:2], scale=c_dtws[:, 1:2]))
                P.wait("scalar", "A", vAx)
                P.run("scalar", lambda e: e.activation(
                    dt2[0][:, :], dt2[0][:, :], AF.Ln, bias=1.0))
                v_A_dt = P.inc("scalar", "A", lambda e: e.activation(
                    dt2[1][:, :], dt2[1][:, :], AF.Ln, bias=1.0))
                P.wait("vector", "A", v_A_dt)
                for pt in range(2):
                    if ch >= 1:
                        P.wait("vector", "T", v_T_tile[(ch - 1, NT - 2 + pt)])
                    P.run("vector", lambda e, pt=pt: e.tensor_tensor(
                        dbx[pt][:, :], dt2[pt][:, :], xs2[pt][:, :], ALU.mult))
                v_V_u = P.inc("vector", "V", lambda e: e.nop())
                P.wait("sync", "V", v_V_u)
                P.wait("sync", "pre", v_pre)
                for pt in range(2):
                    P.inc("sync", "pre", lambda e, pt=pt, lo=lo: e.dma_start(
                        out=u_d[pt * 128:(pt + 1) * 128, lo:lo + LC2],
                        in_=dbx[pt][:, :]), 16)
                    P.inc("sync", "pre", lambda e, pt=pt, lo=lo: e.dma_start(
                        out=dt_d[pt * 128:(pt + 1) * 128, lo:lo + LC2],
                        in_=dt2[pt][:, :]), 16)
                v_u_sp = P.val("pre")

                for t in range(NT):
                    tb = t % 2
                    g = t // 16
                    m = t % 16
                    pk = (ch, t - 2) if t >= 2 else (ch - 1, NT - 2 + t)
                    if pk in v_dbx:
                        P.wait("sync", "V", v_dbx[pk])
                        P.wait("sync", "A", v_A_tile[pk])
                        P.wait("sync", f"ur{tb}", v_sld_rep[pk])
                    if t == 0:
                        P.wait("sync", "pre", v_u_sp)
                    P.inc("sync", f"ur{tb}", lambda e, t=t, lo=lo, tb=tb:
                          e.dma_start(
                              out=u_rep[tb][:, :],
                              in_=u_d[8 * t:8 * t + 8, lo:lo + LC2]
                              .unsqueeze(0).broadcast_to([16, 8, LC2])), 16)
                    v_sld_rep[(ch, t)] = P.inc(
                        "sync", f"ur{tb}", lambda e, t=t, lo=lo, tb=tb:
                        e.dma_start(
                            out=w_rep[tb][:, :],
                            in_=dt_d[8 * t:8 * t + 8, lo:lo + LC2]
                            .unsqueeze(0).broadcast_to([16, 8, LC2])), 16)
                    if pk in v_scan:
                        P.wait("scalar", "V", v_scan[pk])
                    P.wait("scalar", f"ur{tb}", v_sld_rep[(ch, t)])
                    v_A_tile[(ch, t)] = P.inc(
                        "scalar", "A", lambda e, t=t, tb=tb: e.activation(
                            dA2[tb][:, :], w_rep[tb][:, :], AF.Exp,
                            scale=c_A[:, t:t + 1]))
                    P.wait("vector", f"ur{tb}", v_sld_rep[(ch, t)])
                    if t == 0:
                        P.wait("vector", "bmc", vg1[ch])
                    if t <= 1:
                        P.wait("vector", "pre", v_u_sp)
                    if (ch, t) not in ((0, 0), (0, 1)):
                        P.wait("vector", "T", v_T_tile[pk])
                    v_dbx[(ch, t)] = P.inc(
                        "vector", "V", lambda e, tb=tb: e.tensor_tensor(
                            dbx[tb][:, :], u_rep[tb][:, :], bm_rep[:, :],
                            ALU.mult))
                    P.wait("vector", "A", v_A_tile[(ch, t)])
                    if ch == 0:
                        P.run("vector", lambda e, tb=tb: e.tensor_tensor_scan(
                            h2[:, :], dA2[tb][:, :], dbx[tb][:, :], 0.0,
                            ALU.mult, ALU.add))
                    else:
                        P.run("vector", lambda e, tb=tb, t=t: e.tensor_tensor_scan(
                            h2[:, :], dA2[tb][:, :], dbx[tb][:, :],
                            carry[:, t:t + 1], ALU.mult, ALU.add))
                    v_scan[(ch, t)] = P.inc(
                        "vector", "V", lambda e, t=t: e.tensor_copy(
                            carry[:, t:t + 1], h2[:, LC2 - 1:LC2]))
                    v_hc[(ch, t)] = P.inc(
                        "vector", "V", lambda e, tb=tb: e.tensor_tensor(
                            dbx[tb][:, :], h2[:, :], cc_rep[:, :], ALU.mult))
                    pop_steps()
                    P.q_drain(1 << 30)
                    P.wait("tensor", "V", v_hc[(ch, t)])
                    if m == 0 and (ch, g) != (0, 0):
                        pch, pg = (ch, 0) if g == 1 else (ch - 1, 1)
                        P.wait("tensor", "V", v_V_y1[(pch, pg)])
                    for q in range(4):
                        vq = P.inc("tensor", "T",
                                   lambda e, m=m, tb=tb, q=q: e.matmul(
                                       pb[q][:, :],
                                       c_g[:, m * 128:(m + 1) * 128],
                                       dbx[tb][:, q * 512:(q + 1) * 512],
                                       start=(m == 0), stop=(m == 15)))
                    v_T_tile[(ch, t)] = vq
                    if m == 15:
                        P.wait("vector", "T", v_T_tile[(ch, t)])
                        P.wait("vector", "pre", v_u_sp)
                        if ch >= 1:
                            P.wait("vector", f"ys{g}", v_y1g[(ch - 1, g)])
                        for q in range(4):
                            P.run("vector",
                                  lambda e, g=g, q=q:
                                  e.scalar_tensor_tensor(
                                      dt2[g][:, q * 512:(q + 1) * 512],
                                      xs2[g][:, q * 512:(q + 1) * 512],
                                      c_D[:, g:g + 1], pb[q][:, :],
                                      ALU.mult, ALU.add))
                        v_V_y1[(ch, g)] = P.inc(
                            "vector", "V", lambda e, g=g: e.tensor_tensor(
                                xs2[g][:, :], dt2[g][:, :], zs2[g][:, :],
                                ALU.mult))
                        P.wait("sync", "V", v_V_y1[(ch, g)])
                        v_y1g[(ch, g)] = P.inc(
                            "sync", f"ys{g}",
                            lambda e, g=g, ch=ch: e.dma_start(
                                out=a2_i[ch][:, g * 128:(g + 1) * 128, :]
                                .rearrange("j p l -> p j l"),
                                in_=xs2[g][:, :]
                                .rearrange("p (j l) -> p j l", j=8)), 16)

                P.wait("gpsimd", "ys0", v_y1g[(ch, 0)])
                P.wait("gpsimd", "ys1", v_y1g[(ch, 1)])

                def _a2a(e, ch=ch):
                    with allow_ici_all_to_all():
                        return e.collective_compute(
                            "AllToAll", ALU.bypass,
                            replica_groups=[list(range(NCORES))],
                            ins=[a2_i[ch].opt()], outs=[a2_o[ch].opt()])
                st3[("a2a", ch)] = P.inc("gpsimd", "CCX", _a2a)
                if ch >= 1:
                    push_p3b(ch - 1, 0)
                push_p3a(ch)
                if ch >= 1:
                    push_p3b(ch - 1, 1)

            push_p3b(NCH2 - 1, 0)
            push_p3b(NCH2 - 1, 1)
            while p3q:
                fcnt, fn = p3q.popleft()
                fn()
            for lt in range(2):
                mk_ln2(NCH2 - 1, NCH2 - 1, lt)()
            P.q_drain(1 << 30)

            if "u" in taps:
                P.wait("sync", "pre", P.val("pre"))
                P.inc("sync", "tp", lambda e: e.dma_start(
                    out=tap_outs["u"], in_=u_d), 16)
            if "x1" in taps:
                P.wait("sync", "x1s0", P.val("x1s0"))
                P.wait("sync", "x1s1", P.val("x1s1"))
                P.inc("sync", "tp", lambda e: e.dma_start(
                    out=tap_outs["x1"], in_=x1_d), 16)

        P.wait("sync", "os0", P.val("os0"))
        P.wait("sync", "os1", P.val("os1"))
        P.wait("sync", "tp", P.val("tp"))

        with nc.Block() as block:
            for eng in _Plan.ENGINES:
                def _mk(eng):
                    def _run(e):
                        for f in P.ops[eng]:
                            f(e)
                    return _run
                getattr(block, eng)(_mk(eng))

    return nc


def _prep_inputs(inputs):
    x = np.asarray(inputs["L0_attn_x"])
    ipw = np.asarray(inputs["L0_attn_in_proj_w"])
    cw = np.asarray(inputs["L0_attn_conv_w"])
    cb = np.asarray(inputs["L0_attn_conv_b"])
    xpw = np.asarray(inputs["L0_attn_x_proj_w"])
    dtw = np.asarray(inputs["L0_attn_dt_w"])
    dtb = np.asarray(inputs["L0_attn_dt_b"])
    alog = np.asarray(inputs["L0_attn_A_log"])
    Dp = np.asarray(inputs["L0_attn_D"])
    wout = np.asarray(inputs["L0_attn_out_proj_w"])
    Wr = np.asarray(inputs["L0_ffn_Wr"])
    br = np.asarray(inputs["L0_ffn_br"])
    W1 = np.asarray(inputs["L0_ffn_W1"])
    b1 = np.asarray(inputs["L0_ffn_b1"])
    W2 = np.asarray(inputs["L0_ffn_W2"])
    b2 = np.asarray(inputs["L0_ffn_b2"])

    xT = np.ascontiguousarray(x[0].T).astype(np.float32)
    x4 = x[0].reshape(NCH2, NCORES, SL, DM)
    p = np.arange(128)
    g_exp = np.zeros((128, 16, 128), np.float32)
    m_idx = 8 * np.arange(16)[None, :] + (p % 8)[:, None]
    g_exp[p[:, None], np.arange(16)[None, :], m_idx] = 1.0
    g_exp = np.ascontiguousarray(g_exp.reshape(128, 16 * 128))
    ident = np.eye(128, dtype=np.float32)
    ones = np.ones((1, 128), np.float32)

    maps = []
    for c in range(NCORES):
        d0 = c * DC
        dd = d0 + 8 * np.arange(NT)[None, :] + (p % 8)[:, None]
        xpw_p = np.zeros((128, 66), np.float32)
        xpw_p[:, 0:33] = xpw[d0 + p, :]
        xpw_p[:, 33:66] = xpw[d0 + 128 + p, :]
        convw_p = np.zeros((128, 8), np.float32)
        convb_p = np.zeros((128, 2), np.float32)
        d_p = np.zeros((128, 2), np.float32)
        dtw_s = np.zeros((128, 2), np.float32)
        dtb_s = np.zeros((128, 2), np.float32)
        for pt in range(2):
            rows = d0 + pt * 128 + p
            convw_p[:, pt * 4:(pt + 1) * 4] = cw[rows, 0, :]
            convb_p[:, pt] = cb[rows]
            d_p[:, pt] = Dp[rows]
            dtw_s[:, pt] = dtw[rows]
            dtb_s[:, pt] = dtb[rows]
        perm = [c] + [e for e in range(NCORES) if e != c]
        wr_p = np.zeros((128, 64), np.float32)
        for kt in range(8):
            wr_p[:, kt * 8:(kt + 1) * 8] = Wr[kt * 128:(kt + 1) * 128, perm]
        b1_p = np.zeros((128, 16), np.float32)
        for j in range(16):
            b1_p[:, j] = b1[c, j * 128 + p]
        maps.append({
            "xT": xT,
            "x_res": np.ascontiguousarray(x4[:, c].reshape(TC, DM)),
            "winc": np.ascontiguousarray(np.concatenate(
                [ipw[:, d0:d0 + DC], ipw[:, DI + d0:DI + d0 + DC]], axis=1)),
            "xpw_pat": xpw_p,
            "dtw_sm": dtw_s,
            "dtb_sm": dtb_s,
            "alog_pat": np.ascontiguousarray(
                alog[dd, (p // 8)[:, None]]).astype(np.float32),
            "d_pat": d_p,
            "convw_pat": convw_p,
            "convb_pat": convb_p,
            "g_exp": g_exp,
            "ident_in": ident,
            "ones_in": ones,
            "wout_in": wout,
            "wr_pat": wr_p,
            "br_in": br[perm].reshape(1, 8).astype(np.float32),
            "b1_my": b1_p,
            "b2_my": b2[c].reshape(1, DM).astype(np.float32),
            "w1c": np.ascontiguousarray(W1[c]),
            "w2c": np.ascontiguousarray(W2[c]),
        })
    return maps


def kernel(**inputs):
    maps = _prep_inputs(inputs)
    nc = build_program()
    res = run_bass_kernel_spmd(nc, maps, core_ids=list(range(NCORES)))
    stack = np.stack([res.results[c]["out"] for c in range(NCORES)])
    full = stack.reshape(NCORES, NCH2, SL, DM).transpose(1, 0, 2, 3)
    return np.ascontiguousarray(full.reshape(1, L, DM)).astype(np.float32)


# version 14
# speedup vs baseline: 1.0665x; 1.0665x over previous
import numpy as np
from collections import deque
from contextlib import ExitStack

import concourse.bass as bass
import concourse.mybir as mybir
from concourse.bass_utils import run_bass_kernel_spmd
from concourse.aot_env import allow_ici_all_to_all

F32 = mybir.dt.float32
BF16 = mybir.dt.bfloat16
AF = mybir.ActivationFunctionType
ALU = mybir.AluOpType
AXX = mybir.AxisListType.X

NCORES = 8
L = 16384
DM = 1024
DI = 2048
DS = 16
DC = DI // NCORES
TC = L // NCORES
LC1 = 1024
NCH1 = L // LC1
LC2 = 2048
NCH2 = L // LC2
NT = (DC * DS) // 128
SL = 256
EPS = 1e-5
BUDGET = 52


class _Plan:
    ENGINES = ("sync", "gpsimd", "tensor", "vector", "scalar")
    AUTO = {"vector": "V", "scalar": "A", "gpsimd": "G"}

    def __init__(self, nc):
        self.nc = nc
        self.ops = {e: [] for e in self.ENGINES}
        self.sem_h = {}
        self.sem_v = {}
        self.pend = deque()

    def make_sems(self, ctx, names):
        for n in names:
            self.sem_h[n] = ctx.enter_context(self.nc.semaphore(n))
            self.sem_v[n] = 0

    def inc(self, eng, sem, fn, n=1):
        if eng in self.AUTO and sem == self.AUTO[eng]:
            return self._auto(eng, fn)
        h = self.sem_h[sem]
        self.sem_v[sem] += n
        v = self.sem_v[sem]

        def _op(e, fn=fn, h=h, n=n):
            fn(e).then_inc(h, n)
        self.ops[eng].append(_op)
        return v

    def _auto(self, eng, fn):
        sem = self.AUTO[eng]
        h = self.sem_h[sem]
        prev = self.sem_v[sem]
        if prev > 0:
            def _w(e, h=h, prev=prev):
                e.wait_ge(h, prev)
            self.ops[eng].append(_w)
        self.sem_v[sem] = prev + 1

        def _op(e, fn=fn, h=h):
            fn(e).then_inc(h, 1)
        self.ops[eng].append(_op)
        return prev + 1

    def run(self, eng, fn):
        if eng in self.AUTO:
            self._auto(eng, fn)
            return
        self.ops[eng].append(fn)

    def wait(self, eng, sem, v):
        if v <= 0:
            return
        h = self.sem_h[sem]

        def _op(e, h=h, v=v):
            e.wait_ge(h, v)
        self.ops[eng].append(_op)

    def val(self, sem):
        return self.sem_v[sem]

    def q_mm(self, waits, fn):
        self.sem_v["F"] += 1
        v = self.sem_v["F"]
        self.pend.append((list(waits), fn, v))
        return v

    def q_drain(self, k):
        h = self.sem_h["F"]
        n = 0
        while self.pend and n < k:
            waits, fn, v = self.pend.popleft()
            for (sem, val) in waits:
                self.wait("tensor", sem, val)

            def _op(e, fn=fn, h=h):
                fn(e).then_inc(h, 1)
            self.ops["tensor"].append(_op)
            n += 1


def build_program(taps=()):
    nc = bass.Bass("TRN2", target_bir_lowering=False, debug=False)

    def din(name, shape, dtype=F32):
        return nc.dram_tensor(name, shape, dtype, kind="ExternalInput").ap()

    def dout(name, shape, dtype=F32):
        return nc.dram_tensor(name, shape, dtype, kind="ExternalOutput").ap()

    def dint(name, shape, dtype=F32):
        return nc.dram_tensor(name, shape, dtype).ap()

    xT = din("xT", [DM, L])
    x_res = din("x_res", [TC, DM])
    winc = din("winc", [DM, 2 * DC])
    xpw_pat = din("xpw_pat", [128, 2 * 33])
    dtw_sm = din("dtw_sm", [128, 2])
    dtb_sm = din("dtb_sm", [128, 2])
    alog_pat = din("alog_pat", [128, NT])
    d_pat = din("d_pat", [128, 2])
    convw_pat = din("convw_pat", [128, 8])
    convb_pat = din("convb_pat", [128, 2])
    g_exp = din("g_exp", [128, 16 * 128])
    ident_in = din("ident_in", [128, 128])
    ones_in = din("ones_in", [1, 128])
    wout_in = din("wout_in", [DI, DM])
    wr_pat = din("wr_pat", [128, 64])
    br_in = din("br_in", [1, 8])
    b1_my = din("b1_my", [128, 16])
    b2_my = din("b2_my", [1, DM])
    w1c = din("w1c", [DM, DI])
    w2c = din("w2c", [DI, DM])

    out = dout("out", [TC, DM])

    xs_d = dint("xs_d", [DC, L], BF16)
    zs_d = dint("zs_d", [DC, L], BF16)
    u_d = dint("u_d", [DC, L], BF16)
    dt_d = dint("dt_d", [DC, L], BF16)
    x1_d = dint("x1_d", [TC, DM])
    woutb_d = dint("woutb_d", [DI, DM], BF16)
    br_i = [dint(f"br_i{c}", [33, LC2]) for c in range(NCH2)]
    br_o = [dint(f"br_o{c}", [33, LC2]) for c in range(NCH2)]
    a2_i = [dint(f"a2_i{c}", [NCORES, DC, SL], BF16) for c in range(NCH2)]
    a2_o = [dint(f"a2_o{c}", [NCORES, DC, SL], BF16) for c in range(NCH2)]
    ag_i = [dint(f"ag_i{c}", [DM, SL], BF16) for c in range(NCH2)]
    ag_o = [dint(f"ag_o{c}", [NCORES, DM, SL], BF16) for c in range(NCH2)]
    rs_i = [dint(f"rs_i{c}", [LC2, DM], BF16) for c in range(NCH2)]
    rs_o = [dint(f"rs_o{c}", [SL, DM], BF16) for c in range(NCH2)]

    tap_outs = {}
    for t in taps:
        shp, dt = {
            "xs": ([DC, L], BF16), "zs": ([DC, L], BF16), "u": ([DC, L], BF16),
            "x1": ([TC, DM], F32),
        }[t]
        tap_outs[t] = dout("tap_" + t, shp, dt)

    with ExitStack() as octx:
        def sbuf(ctx, name, shape, dtype):
            return ctx.enter_context(nc.sbuf_tensor(name, shape, dtype))

        pb = [octx.enter_context(nc.psum_tensor(f"pb{i}", [128, 512], F32))
              for i in range(8)]

        c_dtws = sbuf(octx, "c_dtws", [128, 2], F32)
        c_dtbs = sbuf(octx, "c_dtbs", [128, 2], F32)
        c_A = sbuf(octx, "c_A", [128, NT], F32)
        c_D = sbuf(octx, "c_D", [128, 2], F32)
        c_cw = sbuf(octx, "c_cw", [128, 8], F32)
        c_cb = sbuf(octx, "c_cb", [128, 2], F32)
        c_id = sbuf(octx, "c_id", [128, 128], F32)
        c_ones = sbuf(octx, "c_ones", [1, 128], BF16)
        c_wr = sbuf(octx, "c_wr", [128, 64], BF16)
        c_br = sbuf(octx, "c_br", [1, 8], BF16)
        c_b1 = sbuf(octx, "c_b1", [128, 16], F32)
        c_b2 = sbuf(octx, "c_b2", [1, DM], BF16)
        c_xpw = sbuf(octx, "c_xpw", [128, 2 * 33], BF16)
        c_g = sbuf(octx, "c_g", [128, 16 * 128], BF16)
        c_k = sbuf(octx, "c_k", [128, 3], F32)
        carry = sbuf(octx, "carry", [128, NT], F32)
        w1e = sbuf(octx, "w1e", [128, 8 * 2048], BF16)
        w2e = sbuf(octx, "w2e", [128, 16 * 1024], BF16)

        P = _Plan(nc)
        P.make_sems(octx, [
            "boot", "boot2", "xt0", "xt1", "sp1", "pre", "bmc", "brs",
            "ur0", "ur1", "ys0", "ys1", "y1t", "xres0", "xres1",
            "x1s0", "x1s1", "wol", "wo2", "wl3", "wl4", "shp", "tp",
            "wsc0", "wsc1", "xfl", "xf2", "smc", "rst", "os0", "os1",
            "T", "F", "V", "A", "G", "CCR", "CCX", "CCG", "CCS"])

        for dst, src in [(c_dtws, dtw_sm), (c_dtbs, dtb_sm), (c_D, d_pat),
                         (c_cw, convw_pat), (c_cb, convb_pat),
                         (c_id, ident_in), (c_b1, b1_my)]:
            P.inc("sync", "boot", lambda e, d=dst, s=src: e.dma_start(out=d[:, :], in_=s), 16)
        P.inc("sync", "boot", lambda e: e.dma_start(out=c_A[:, :], in_=alog_pat), 16)
        v_boot = P.val("boot")
        for dst, src in [(c_ones, ones_in), (c_wr, wr_pat), (c_br, br_in),
                         (c_b2, b2_my), (c_xpw, xpw_pat), (c_g, g_exp)]:
            P.inc("gpsimd", "boot2", lambda e, d=dst, s=src: e.dma_start(out=d[:, :], in_=s), 16)
        for kt in range(8):
            P.inc("gpsimd", "boot2", lambda e, kt=kt: e.dma_start(
                out=w1e[:, kt * 2048:(kt + 1) * 2048],
                in_=w1c[kt * 128:(kt + 1) * 128, :]), 16)
        for ft in range(16):
            P.inc("gpsimd", "boot2", lambda e, ft=ft: e.dma_start(
                out=w2e[:, ft * 1024:(ft + 1) * 1024],
                in_=w2c[ft * 128:(ft + 1) * 128, :]), 16)
        v_boot2 = P.val("boot2")

        P.wait("vector", "boot", v_boot)
        P.run("vector", lambda e: e.memset(c_k[:, 0:1], EPS))
        P.run("vector", lambda e: e.memset(c_k[:, 1:2], -0.5))
        v_ck = P.inc("vector", "V", lambda e: e.memset(c_k[:, 2:3], -1.0))
        P.wait("scalar", "boot", v_boot)
        P.run("scalar", lambda e: e.activation(c_A[:, :], c_A[:, :], AF.Exp))
        P.wait("scalar", "V", v_ck)
        v_cA = P.inc("scalar", "A", lambda e: e.mul(c_A[:, :], c_A[:, :], c_k[:, 2:3]))

        with ExitStack() as sA1:
            c_win = sbuf(sA1, "c_win", [128, 8 * 512], BF16)
            xTb = [sbuf(sA1, f"xTb{i}", [128, 8 * LC1], BF16) for i in range(2)]
            x_buf = [sbuf(sA1, f"x_buf{i}", [128, 3 + LC1], F32) for i in range(2)]
            cv = [sbuf(sA1, f"cv{i}", [128, LC1], F32) for i in range(2)]
            xs_c = [sbuf(sA1, f"xs_c{i}", [128, LC1], BF16) for i in range(2)]
            zs_c = [sbuf(sA1, f"zs_c{i}", [128, LC1], BF16) for i in range(2)]
            bcdp = sbuf(sA1, "bcdp", [33, L], F32)
            wsc = [sbuf(sA1, f"wsc{i}", [128, 1024], BF16) for i in range(2)]

            P.wait("gpsimd", "boot2", v_boot2)
            for kt in range(8):
                P.inc("gpsimd", "boot2", lambda e, kt=kt: e.dma_start(
                    out=c_win[:, kt * 512:(kt + 1) * 512],
                    in_=winc[kt * 128:(kt + 1) * 128, :]), 16)
            v_win = P.val("boot2")

            P.run("vector", lambda e: e.memset(x_buf[0][:, 0:3], 0.0))
            P.inc("vector", "V", lambda e: e.memset(x_buf[1][:, 0:3], 0.0))

            v_wo_st = [0, 0]
            wo_i = 0
            v_T_ip = {}
            v_A_ev = {}
            v_V_conv = {}
            v_A_xs = {}
            v_T_xp = {}
            v_V_bc = {}
            v_sp = {}
            v_ld = {}
            v_ar = {}

            v_wsc = {0: 0, 1: 0}
            for ch in range(NCH1):
                cb = ch % 2
                if ch >= 2:
                    P.wait("gpsimd", "T", v_T_ip[(ch - 2, 3)])
                    P.wait("gpsimd", f"xt{cb}", v_ld[ch - 2])
                for kt in range(8):
                    P.inc("gpsimd", f"xt{cb}",
                          lambda e, kt=kt, ch=ch, cb=cb: e.dma_start(
                              out=xTb[cb][:, kt * LC1:(kt + 1) * LC1],
                              in_=xT[kt * 128:(kt + 1) * 128,
                                     ch * LC1:(ch + 1) * LC1]), 16)
                v_ld[ch] = P.val(f"xt{cb}")
                for _ in range(2):
                    if wo_i < 16:
                        wb = wo_i % 2
                        wsem = "wol" if wb == 0 else "wo2"
                        if v_wo_st[wb]:
                            P.wait("gpsimd", wsem, v_wo_st[wb])
                        P.wait("gpsimd", f"wsc{wb}", v_wsc[wb])
                        vg = P.inc("gpsimd", f"wsc{wb}",
                                   lambda e, wb=wb, wo_i=wo_i: e.dma_start(
                                       out=wsc[wb][:, :],
                                       in_=wout_in[wo_i * 128:(wo_i + 1) * 128, :]), 16)
                        v_wsc[wb] = vg
                        P.wait("sync", f"wsc{wb}", vg)
                        v_wo_st[wb] = P.inc(
                            "sync", wsem,
                            lambda e, wb=wb, wo_i=wo_i: e.dma_start(
                                out=woutb_d[wo_i * 128:(wo_i + 1) * 128, :],
                                in_=wsc[wb][:, :]), 16)
                        wo_i += 1
                v_wo_done = (v_wo_st[0], v_wo_st[1])

                if ch < 2:
                    P.wait("tensor", "boot2", v_win)
                P.wait("tensor", f"xt{cb}", v_ld[ch])
                for ct in range(4):
                    pp = (pb[0], pb[1]) if ct % 2 == 0 else (pb[2], pb[3])
                    if (ch, ct) not in ((0, 0), (0, 1)):
                        prev = (ch, ct - 2) if ct >= 2 else (ch - 1, ct + 2)
                        P.wait("tensor", "A", v_A_ev[prev])
                    for kt in range(8):
                        for nh in range(2):
                            P.inc("tensor", "T",
                                  lambda e, kt=kt, ct=ct, cb=cb, pq=pp[nh], nh=nh:
                                  e.matmul(
                                      pq[:, :],
                                      c_win[:, kt * 512 + ct * 128:
                                            kt * 512 + (ct + 1) * 128],
                                      xTb[cb][:, kt * LC1 + nh * 512:
                                              kt * LC1 + (nh + 1) * 512],
                                      start=(kt == 0), stop=(kt == 7)))
                    v_T_ip[(ch, ct)] = P.val("T")

                    P.wait("scalar", "T", v_T_ip[(ch, ct)])
                    if ct < 2:
                        if ch >= 1:
                            P.wait("scalar", "V", v_V_conv[(ch - 1, ct)])
                        for nh in range(2):
                            v_A_ev[(ch, ct)] = P.inc(
                                "scalar", "A",
                                lambda e, ct=ct, nh=nh, pq=pp[nh]: e.copy(
                                    x_buf[ct][:, 3 + nh * 512:3 + (nh + 1) * 512],
                                    pq[:, :]))
                    else:
                        pt = ct - 2
                        if ch >= 1:
                            P.wait("scalar", "sp1", v_sp[ch - 1])
                        for nh in range(2):
                            v_A_ev[(ch, ct)] = P.inc(
                                "scalar", "A",
                                lambda e, pt=pt, nh=nh, pq=pp[nh]: e.activation(
                                    zs_c[pt][:, nh * 512:(nh + 1) * 512],
                                    pq[:, :], AF.Silu))
                for pt in range(2):
                    P.wait("vector", "A", v_A_ev[(ch, pt)])
                    P.run("vector", lambda e, pt=pt: e.tensor_scalar(
                        cv[pt][:, :], x_buf[pt][:, 0:LC1],
                        c_cw[:, pt * 4:pt * 4 + 1], None, ALU.mult))
                    for k in (1, 2, 3):
                        P.run("vector", lambda e, pt=pt, k=k: e.scalar_tensor_tensor(
                            cv[pt][:, :], x_buf[pt][:, k:k + LC1],
                            c_cw[:, pt * 4 + k:pt * 4 + k + 1],
                            cv[pt][:, :], ALU.mult, ALU.add))
                    v_V_conv[(ch, pt)] = P.inc(
                        "vector", "V",
                        lambda e, pt=pt: e.tensor_copy(
                            x_buf[pt][:, 0:3], x_buf[pt][:, LC1:LC1 + 3]))
                    P.wait("scalar", "V", v_V_conv[(ch, pt)])
                    if ch >= 1:
                        P.wait("scalar", "sp1", v_sp[ch - 1])
                    v_A_xs[(ch, pt)] = P.inc(
                        "scalar", "A",
                        lambda e, pt=pt: e.activation(
                            xs_c[pt][:, :], cv[pt][:, :], AF.Silu,
                            bias=c_cb[:, pt:pt + 1]))
                P.wait("sync", "A", max(v_A_xs[(ch, 1)], v_A_ev[(ch, 3)]))
                if ch >= 1:
                    P.wait("sync", "sp1", v_sp[ch - 1])
                for pt in range(2):
                    P.inc("sync", "sp1", lambda e, pt=pt, ch=ch: e.dma_start(
                        out=xs_d[pt * 128:(pt + 1) * 128,
                                 ch * LC1:(ch + 1) * LC1],
                        in_=xs_c[pt][:, :]), 16)
                    P.inc("sync", "sp1", lambda e, pt=pt, ch=ch: e.dma_start(
                        out=zs_d[pt * 128:(pt + 1) * 128,
                                 ch * LC1:(ch + 1) * LC1],
                        in_=zs_c[pt][:, :]), 16)
                v_sp[ch] = P.val("sp1")

                P.wait("tensor", "A", v_A_xs[(ch, 1)])
                if ch >= 1:
                    P.wait("tensor", "V", v_V_bc[ch - 1])
                for kt in range(2):
                    for nh in range(2):
                        P.inc("tensor", "T", lambda e, kt=kt, nh=nh: e.matmul(
                            pb[4 + nh][0:33, :],
                            c_xpw[:, kt * 33:(kt + 1) * 33],
                            xs_c[kt][:, nh * 512:(nh + 1) * 512],
                            start=(kt == 0), stop=(kt == 1)))
                v_T_xp[ch] = P.val("T")
                P.wait("vector", "T", v_T_xp[ch])
                for nh in range(2):
                    v_V_bc[ch] = P.inc(
                        "vector", "V", lambda e, ch=ch, nh=nh: e.tensor_copy(
                            bcdp[:, ch * LC1 + nh * 512:
                                 ch * LC1 + (nh + 1) * 512],
                            pb[4 + nh][0:33, :]))

                if ch % 2 == 1:
                    c2 = ch // 2
                    P.wait("sync", "V", v_V_bc[ch])
                    P.wait("sync", "brs", 16 * c2)
                    v_sp1 = P.inc("sync", "brs",
                                  lambda e, c2=c2: e.dma_start(
                                      out=br_i[c2],
                                      in_=bcdp[:, c2 * LC2:(c2 + 1) * LC2]), 16)
                    P.wait("gpsimd", "brs", v_sp1)
                    P.wait("gpsimd", "sp1", v_sp[ch])
                    v_ar[c2] = P.inc("gpsimd", "CCR",
                                     lambda e, c2=c2: e.collective_compute(
                                         "AllReduce", ALU.add,
                                         replica_groups=[list(range(NCORES))],
                                         ins=[br_i[c2].opt()],
                                         outs=[br_o[c2].opt()]))

            if "xs" in taps:
                P.wait("sync", "sp1", P.val("sp1"))
                P.inc("sync", "tp", lambda e: e.dma_start(
                    out=tap_outs["xs"], in_=xs_d), 16)
            if "zs" in taps:
                P.wait("sync", "sp1", P.val("sp1"))
                P.inc("sync", "tp", lambda e: e.dma_start(
                    out=tap_outs["zs"], in_=zs_d), 16)

            v_ph1 = {s: P.val(s) for s in ("T", "V", "A")}

        with ExitStack() as sA2:
            bm_rep = sbuf(sA2, "bm_rep", [128, LC2], BF16)
            cc_rep = sbuf(sA2, "cc_rep", [128, LC2], BF16)
            xs2 = [sbuf(sA2, f"xs2_{i}", [128, LC2], BF16) for i in range(2)]
            zs2 = [sbuf(sA2, f"zs2_{i}", [128, LC2], BF16) for i in range(2)]
            dt2 = [sbuf(sA2, f"dt2_{i}", [128, LC2], BF16) for i in range(2)]
            w_rep = [sbuf(sA2, f"w_rep{i}", [128, LC2], BF16) for i in range(2)]
            dA2 = [sbuf(sA2, f"dA2_{i}", [128, LC2], F32) for i in range(2)]
            u_rep = [sbuf(sA2, f"u_rep{i}", [128, LC2], BF16) for i in range(2)]
            dbx = [sbuf(sA2, f"dbx{i}", [128, LC2], BF16) for i in range(2)]
            h2 = sbuf(sA2, "h2", [128, LC2], BF16)
            y1T = sbuf(sA2, "y1T", [128, 16 * SL], BF16)
            wot = [sbuf(sA2, f"wot{i}", [128, 1024], BF16) for i in range(2)]
            xr = [sbuf(sA2, f"xr{i}", [128, 1024], F32) for i in range(2)]
            sq3 = sbuf(sA2, "sq3", [128, 512], F32)
            x1T = sbuf(sA2, "x1T", [128, 8 * SL], BF16)
            xfT = sbuf(sA2, "xfT", [128, 8 * 1024], BF16)
            hb3 = sbuf(sA2, "hb3", [128, 16 * SL], BF16)
            rsb = [sbuf(sA2, f"rsb{i}", [128, 1024], BF16) for i in range(2)]
            rs2 = [sbuf(sA2, f"rs2_{i}", [128, 1024], BF16) for i in range(2)]
            gates = sbuf(sA2, "gates", [128, 16 * 8], F32)
            gl2 = sbuf(sA2, "gl2", [128, 64], F32)
            rtc = sbuf(sA2, "rtc", [128, 40], F32)
            lnc = sbuf(sA2, "lnc", [128, 48], F32)

            for eng in _Plan.ENGINES:
                for s in ("T", "V", "A"):
                    P.wait(eng, s, v_ph1[s])
            P.wait("tensor", "boot2", v_boot2)
            P.wait("scalar", "A", v_cA)

            v_A_tile = {}
            v_dbx = {}
            v_hc = {}
            v_scan = {}
            v_T_tile = {}
            v_V_y1 = {}
            v_sld_rep = {}
            v_y1g = {}
            vg1 = {}
            st3 = {}
            p3q = deque()

            def pop_steps(budget=BUDGET):
                n = 0
                while p3q and n < budget:
                    fcnt, fn = p3q.popleft()
                    fn()
                    n += max(fcnt, 1)

            def mk_y1t(c):
                def _s(c=c):
                    P.wait("gpsimd", "CCX", st3[("a2a", c)])
                    P.wait("gpsimd", "y1t", P.val("y1t"))
                    if c >= 1:
                        P.wait("gpsimd", "F", st3[("opjF", c - 1)])
                    for kt in range(16):
                        P.inc("gpsimd", "y1t", lambda e, kt=kt, c=c: e.dma_start(
                            out=y1T[:, kt * SL:(kt + 1) * SL],
                            in_=a2_o[c][kt // 2,
                                        (kt % 2) * 128:(kt % 2 + 1) * 128, :]), 16)
                    st3[("y1tl", c)] = P.val("y1t")
                return _s

            def mk_opj(c, kt):
                def _s(c=c, kt=kt):
                    wb = kt % 2
                    wsem = "wl3" if wb == 0 else "wl4"
                    if st3.get(("wotmm", wb), 0):
                        P.wait("gpsimd", "F", st3[("wotmm", wb)])
                    P.wait("gpsimd", "wol" if wb == 0 else "wo2",
                           v_wo_done[wb])
                    P.wait("gpsimd", wsem, st3.get(("wolv", wb), 0))
                    P.inc("gpsimd", wsem, lambda e, kt=kt, wb=wb: e.dma_start(
                        out=wot[wb][:, :],
                        in_=woutb_d[kt * 128:(kt + 1) * 128, :]), 16)
                    v_wl = P.val(wsem)
                    st3[("wolv", wb)] = v_wl
                    wts = [("y1t", st3[("y1tl", c)]), (wsem, v_wl)]
                    if kt == 0 and c >= 1:
                        wts.append(("V", st3.get(("rtrV",), 0)))
                        wts.append(("A", st3.get(("rse_last",), 0)))
                    v = 0
                    for lt in range(2):
                        for nh in range(2):
                            v = P.q_mm(wts,
                                       lambda e, kt=kt, lt=lt, nh=nh, wb=wb,
                                       pq=pb[4 + lt * 2 + nh]:
                                       e.matmul(
                                           pq[:, :],
                                           y1T[:, kt * SL + lt * 128:
                                               kt * SL + (lt + 1) * 128],
                                           wot[wb][:, nh * 512:(nh + 1) * 512],
                                           start=(kt == 0), stop=(kt == 15)))
                            wts = []
                        st3[("opjf", c, lt)] = v
                    st3[("wotmm", wb)] = v
                    if kt == 15:
                        st3[("opjF", c)] = v
                return _s

            def _ln_math(o, buf):
                P.run("vector", lambda e: e.reduce_sum(
                    lnc[:, o:o + 1], buf[:, :], axis=AXX))
                P.run("vector", lambda e: e.tensor_tensor(
                    sq3[:, :], buf[:, 0:512], buf[:, 0:512], ALU.mult))
                P.run("vector", lambda e: e.reduce_sum(
                    lnc[:, o + 1:o + 2], sq3[:, :], axis=AXX))
                P.run("vector", lambda e: e.tensor_tensor(
                    sq3[:, :], buf[:, 512:1024], buf[:, 512:1024], ALU.mult))
                P.run("vector", lambda e: e.reduce_sum(
                    lnc[:, o + 10:o + 11], sq3[:, :], axis=AXX))
                P.run("vector", lambda e: e.tensor_tensor(
                    lnc[:, o + 1:o + 2], lnc[:, o + 1:o + 2],
                    lnc[:, o + 10:o + 11], ALU.add))
                P.run("vector", lambda e: e.tensor_scalar(
                    lnc[:, o + 2:o + 3], lnc[:, o:o + 1], 1.0 / DM, None, ALU.mult))
                P.run("vector", lambda e: e.tensor_scalar(
                    lnc[:, o + 3:o + 4], lnc[:, o + 1:o + 2], 1.0 / DM, None, ALU.mult))
                P.run("vector", lambda e: e.tensor_tensor(
                    lnc[:, o + 4:o + 5], lnc[:, o + 2:o + 3],
                    lnc[:, o + 2:o + 3], ALU.mult))
                P.run("vector", lambda e: e.tensor_tensor(
                    lnc[:, o + 5:o + 6], lnc[:, o + 3:o + 4],
                    lnc[:, o + 4:o + 5], ALU.subtract))
                v_Vvar = P.inc("vector", "V", lambda e: e.tensor_scalar(
                    lnc[:, o + 5:o + 6], lnc[:, o + 5:o + 6], EPS, None, ALU.add))
                P.wait("scalar", "V", v_Vvar)
                P.run("scalar", lambda e: e.activation(
                    lnc[:, o + 6:o + 7], lnc[:, o + 5:o + 6], AF.Ln))
                v_Ai = P.inc("scalar", "A", lambda e: e.activation(
                    lnc[:, o + 7:o + 8], lnc[:, o + 6:o + 7],
                    AF.Exp, scale=c_k[:, 1:2]))
                P.wait("vector", "A", v_Ai)
                P.run("vector", lambda e: e.tensor_tensor(
                    lnc[:, o + 8:o + 9], lnc[:, o + 2:o + 3],
                    lnc[:, o + 7:o + 8], ALU.mult))
                v_Vnb = P.inc("vector", "V", lambda e: e.tensor_scalar(
                    lnc[:, o + 9:o + 10], lnc[:, o + 8:o + 9],
                    c_k[:, 2:3], None, ALU.mult))
                P.wait("scalar", "V", v_Vnb)
                v_n = P.inc("scalar", "A", lambda e: e.activation(
                    buf[:, :], buf[:, :], AF.Identity,
                    bias=lnc[:, o + 9:o + 10], scale=lnc[:, o + 7:o + 8]))
                return v_n

            def mk_ln1(c, lt):
                def _s(c=c, lt=lt):
                    lb = lt
                    o = lt * 12
                    tok0 = c * SL + lt * 128
                    P.wait("scalar", f"os{lb}", st3.get(("ost", c - 2, lt), 0))
                    P.wait("scalar", f"x1s{lb}", st3.get(("x1s", c - 1, lt), 0))
                    P.wait("scalar", "F", st3.get(("trF", c - 1), 0))
                    P.wait("scalar", f"xres{lb}", P.val(f"xres{lb}"))
                    P.inc("scalar", f"xres{lb}",
                          lambda e, tok0=tok0, lb=lb: e.dma_start(
                              out=xr[lb][:, :],
                              in_=x_res[tok0:tok0 + 128, :]), 16)
                    v_xr = P.val(f"xres{lb}")
                    P.wait("vector", "F", st3[("opjf", c, lt)])
                    P.wait("vector", f"xres{lb}", v_xr)
                    P.wait("vector", "A", st3.get(("x1v", c - 1, lt), 0))
                    v_add = 0
                    for nh in range(2):
                        v_add = P.inc("vector", "V",
                                      lambda e, lb=lb, nh=nh,
                                      pq=pb[4 + lt * 2 + nh]:
                                      e.tensor_tensor(
                                          xr[lb][:, nh * 512:(nh + 1) * 512],
                                          pq[:, :],
                                          xr[lb][:, nh * 512:(nh + 1) * 512],
                                          ALU.add))
                    st3[("add", c, lt)] = v_add
                    v_x1 = _ln_math(o, xr[lb])
                    st3[("x1v", c, lt)] = v_x1
                    P.wait("scalar", "A", v_x1)
                    P.wait("scalar", f"x1s{lb}", P.val(f"x1s{lb}"))
                    P.inc("scalar", f"x1s{lb}",
                          lambda e, tok0=tok0, lb=lb: e.dma_start(
                              out=x1_d[tok0:tok0 + 128, :], in_=xr[lb][:, :]), 16)
                    st3[("x1s", c, lt)] = P.val(f"x1s{lb}")
                return _s

            def mk_tr(c, lt):
                def _s(c=c, lt=lt):
                    wts = [("A", st3[("x1v", c, lt)]),
                           ("V", st3[("add", c, lt)])]
                    v = 0
                    for j in range(8):
                        v = P.q_mm(wts, lambda e, j=j, lt=lt,
                                   pq=pb[4 + lt * 2 + j // 4]: e.transpose(
                                       pq[:, (j % 4) * 128:(j % 4 + 1) * 128],
                                       xr[lt][:, j * 128:(j + 1) * 128],
                                       c_id[:, :]))
                        wts = []
                    st3[("trF", c)] = v
                    P.wait("scalar", "F", v)
                    if c >= 1:
                        P.wait("scalar", "shp", st3[("agship", c - 1)])
                    v_te = 0
                    for half in range(2):
                        v_te = P.inc("scalar", "A",
                                     lambda e, lt=lt, half=half,
                                     pq=pb[4 + lt * 2 + half]: e.copy(
                                         x1T[:, :]
                                         .rearrange("p (j s) -> p j s", j=8)
                                         [:, half * 4:(half + 1) * 4,
                                          lt * 128:(lt + 1) * 128],
                                         pq[:, :].rearrange(
                                             "p (j f) -> p j f", j=4)))
                    st3[("te", c, lt)] = v_te
                return _s

            def mk_ship(c):
                def _s(c=c):
                    P.wait("scalar", "A",
                           max(st3[("te", c, 0)], st3[("te", c, 1)]))
                    P.wait("scalar", "shp", P.val("shp"))
                    P.inc("scalar", "shp", lambda e, c=c: e.dma_start(
                        out=ag_i[c].rearrange("(j p) s -> p j s", j=8),
                        in_=x1T[:, :].rearrange("p (j s) -> p j s", j=8)), 16)
                    st3[("agship", c)] = P.val("shp")
                    P.wait("gpsimd", "shp", st3[("agship", c)])
                    st3[("ag", c)] = P.inc(
                        "gpsimd", "CCG", lambda e, c=c: e.collective_compute(
                            "AllGather", ALU.bypass,
                            replica_groups=[list(range(NCORES))],
                            ins=[ag_i[c].opt()], outs=[ag_o[c].opt()]))
                return _s

            def mk_ln2(c, gk, lt):
                def _s(c=c, gk=gk, lt=lt):
                    lb = lt
                    o = 24 + lt * 12
                    tok0 = c * SL + lt * 128
                    P.wait("scalar", "CCS", st3[("rs", c)])
                    P.wait("scalar", "F", st3[("trF", gk)])
                    P.wait("scalar", f"x1s{lb}", st3[("x1s", gk, lt)])
                    if c >= 1:
                        P.wait("scalar", f"os{lb}", st3[("ost", c - 1, lt)])
                    P.wait("scalar", f"xres{lb}", P.val(f"xres{lb}"))
                    P.inc("scalar", f"xres{lb}", lambda e, c=c, lt=lt, lb=lb:
                          e.dma_start(out=rs2[lb][:, :],
                                      in_=rs_o[c][lt * 128:(lt + 1) * 128, :]), 16)
                    P.inc("scalar", f"xres{lb}", lambda e, tok0=tok0, lb=lb:
                          e.dma_start(out=xr[lb][:, :],
                                      in_=x1_d[tok0:tok0 + 128, :]), 16)
                    v_l2 = P.val(f"xres{lb}")
                    P.wait("vector", f"xres{lb}", v_l2)
                    P.wait("vector", "A", st3.get(("ln2v", c - 1, lt), 0))
                    P.run("vector", lambda e, lb=lb: e.tensor_tensor(
                        xr[lb][:, :], xr[lb][:, :], rs2[lb][:, :], ALU.add))
                    v_o = _ln_math(o, xr[lb])
                    st3[("ln2v", c, lt)] = v_o
                    P.wait("scalar", "A", v_o)
                    P.inc("scalar", f"os{lb}", lambda e, tok0=tok0, lb=lb:
                          e.dma_start(out=out[tok0:tok0 + 128, :],
                                      in_=xr[lb][:, :]), 16)
                    st3[("ost", c, lt)] = P.val(f"os{lb}")
                return _s

            def mk_xflq(c, hf, q):
                def _s(c=c, hf=hf, q=q):
                    sem = "xfl" if q % 2 == 0 else "xf2"
                    if q == 0:
                        P.wait("gpsimd", "CCG", st3[("ag", c)])
                    P.wait("gpsimd", "F", st3.get(("m1l", q), 0))
                    P.wait("gpsimd", sem, P.val(sem))
                    for kt in range(8):
                        P.inc("gpsimd", sem,
                              lambda e, kt=kt, c=c, hf=hf, q=q: e.dma_start(
                                  out=xfT[:, kt * 1024 + q * 256:
                                          kt * 1024 + (q + 1) * 256],
                                  in_=ag_o[c][4 * hf + q,
                                              kt * 128:(kt + 1) * 128, :]), 16)
                    st3[("xflv", q)] = (sem, P.val(sem))
                return _s

            def mk_rtr(c, hf, lt, gc):
                def _s(c=c, hf=hf, lt=lt, gc=gc):
                    wts = [st3[("xflv", lt // 2)]]
                    if lt == 0:
                        wts.append(("boot2", v_boot2))
                        wts.append(("A", st3.get(("te", gc, 0), 0)))
                        wts.append(("V", st3.get(("rtrV",), 0)))
                    ppo = lt * 8
                    for kt in range(8):
                        P.q_mm(wts, lambda e, kt=kt, ppo=ppo, lt=lt: e.matmul(
                            pb[4][:, ppo:ppo + 8],
                            xfT[:, kt * 1024 + lt * 128:
                                kt * 1024 + (lt + 1) * 128],
                            c_wr[:, kt * 8:(kt + 1) * 8],
                            start=(kt == 0), stop=False))
                        wts = []
                    st3[("rfv",)] = P.q_mm([], lambda e, ppo=ppo: e.matmul(
                        pb[4][:, ppo:ppo + 8], c_ones[:1, :], c_br[:1, :],
                        start=False, stop=True))
                return _s

            def mk_gates(c, hf):
                def _s(c=c, hf=hf):
                    r3 = lambda ap: ap.rearrange("p (l e) -> p l e", e=8)
                    bc = lambda col: rtc[:, col:col + 8].unsqueeze(2)\
                        .broadcast_to([128, 8, 8])
                    go = hf * 64
                    P.wait("vector", "F", st3[("rfv",)])
                    P.wait("vector", "A", st3.get(("rse_last",), 0))
                    v_cp = P.inc("vector", "V", lambda e: e.tensor_copy(
                        sq3[:, 0:64], pb[4][:, 0:64]))
                    st3[("rtrV",)] = v_cp
                    P.run("vector", lambda e: e.reduce_max(
                        rtc[:, 0:8].unsqueeze(2), r3(sq3[:, 0:64]), axis=AXX))
                    P.run("vector", lambda e, go=go: e.tensor_tensor(
                        r3(gates[:, go:go + 64]), r3(sq3[:, 0:64]), bc(0),
                        ALU.is_equal))
                    P.run("vector", lambda e, go=go: e.scalar_tensor_tensor(
                        gl2[:, 0:64], gates[:, go:go + 64], -1e30, sq3[:, 0:64],
                        ALU.mult, ALU.add))
                    P.run("vector", lambda e: e.reduce_max(
                        rtc[:, 8:16].unsqueeze(2), r3(gl2[:, 0:64]), axis=AXX))
                    v_Vd = P.inc("vector", "V", lambda e: e.tensor_tensor(
                        rtc[:, 16:24], rtc[:, 8:16], rtc[:, 0:8], ALU.subtract))
                    P.wait("scalar", "V", v_Vd)
                    v_Ae = P.inc("scalar", "A", lambda e: e.activation(
                        rtc[:, 24:32], rtc[:, 16:24], AF.Exp))
                    P.wait("vector", "A", v_Ae)
                    P.run("vector", lambda e: e.tensor_scalar(
                        rtc[:, 32:40], rtc[:, 24:32], 1.0, None, ALU.add))
                    P.run("vector", lambda e: e.reciprocal(
                        rtc[:, 32:40], rtc[:, 32:40]))
                    P.run("vector", lambda e: e.tensor_tensor(
                        rtc[:, 24:32], rtc[:, 24:32], rtc[:, 32:40], ALU.mult))
                    P.run("vector", lambda e, go=go: e.tensor_tensor(
                        r3(gates[:, go:go + 64]), r3(gates[:, go:go + 64]),
                        bc(32), ALU.mult))
                    P.run("vector", lambda e: e.tensor_tensor(
                        r3(gl2[:, 0:64]), r3(gl2[:, 0:64]), bc(8), ALU.is_equal))
                    P.run("vector", lambda e: e.tensor_tensor(
                        r3(gl2[:, 0:64]), r3(gl2[:, 0:64]), bc(24), ALU.mult))
                    v_gv = P.inc("vector", "V", lambda e, go=go: e.tensor_tensor(
                        gates[:, go:go + 64], gl2[:, 0:64], gates[:, go:go + 64],
                        ALU.add))
                    st3[("gv", c, hf)] = v_gv
                return _s

            def mk_mm1(c, hf, q, ft, gc):
                def _s(c=c, hf=hf, q=q, ft=ft, gc=gc):
                    fb = ft % 2
                    q0 = q * 256
                    wts = []
                    if ft == 0:
                        wts.append(st3[("xflv", q)])
                    if q == 0 and ft <= 1:
                        if ft == 0:
                            wts.append(("V", st3.get(("rtrV",), 0)))
                        wts.append(("A", max(st3.get(("te", gc, 0), 0),
                                             st3.get(("h3A", fb), 0))))
                    else:
                        prevh = st3.get(("h3A", fb), 0)
                        if prevh:
                            wts.append(("A", prevh))
                    v = 0
                    for kt in range(8):
                        v = P.q_mm(wts, lambda e, kt=kt, ft=ft, q0=q0,
                                   pq=pb[4 + fb]:
                                   e.matmul(
                                       pq[:, 0:256],
                                       w1e[:, kt * 2048 + ft * 128:
                                           kt * 2048 + (ft + 1) * 128],
                                       xfT[:, kt * 1024 + q0:
                                           kt * 1024 + q0 + 256],
                                       start=(kt == 0), stop=(kt == 7)))
                        wts = []
                    P.wait("scalar", "F", v)
                    v_h3 = P.inc("scalar", "A", lambda e, ft=ft, pq=pb[4 + fb]:
                                 e.activation(
                                     hb3[:, ft * SL:(ft + 1) * SL],
                                     pq[:, 0:256],
                                     AF.Relu, bias=c_b1[:, ft:ft + 1]))
                    st3[("h3A", fb)] = v_h3
                    st3[("h3", ft)] = v_h3
                    if ft == 15:
                        st3[("m1l", q)] = v
                return _s

            def mk_mm2(c, hf, q, tt, gc):
                def _s(c=c, hf=hf, q=q, tt=tt, gc=gc):
                    g0 = st3.get(("rsePD",), 0)
                    if q == 0 and tt == 0:
                        g0 = max(g0, st3.get(("te", gc, 1), 0))
                    v = 0
                    for ft in range(16):
                        wts = [("A", max(st3[("h3", ft)], g0 if ft == 0 else 0))]
                        for nh in range(2):
                            v = P.q_mm(wts,
                                       lambda e, ft=ft, tt=tt, nh=nh,
                                       pq=pb[6 + nh]: e.matmul(
                                           pq[:, :],
                                           hb3[:, ft * SL + tt * 128:
                                               ft * SL + (tt + 1) * 128],
                                           w2e[:, ft * 1024 + nh * 512:
                                               ft * 1024 + (nh + 1) * 512],
                                           start=(ft == 0), stop=False))
                            wts = []
                    for nh in range(2):
                        v = P.q_mm([], lambda e, nh=nh, pq=pb[6 + nh]: e.matmul(
                            pq[:, :], c_ones[:1, :],
                            c_b2[:1, nh * 512:(nh + 1) * 512],
                            start=False, stop=True))
                    glt = hf * 8 + q * 2 + tt
                    P.wait("scalar", "F", v)
                    P.wait("scalar", "V", st3[("gv", c, hf)])
                    P.wait("scalar", "rst", st3.get(("rstv", tt), 0))
                    v_rse = 0
                    for nh in range(2):
                        v_rse = P.inc("scalar", "A",
                                      lambda e, tt=tt, glt=glt, nh=nh,
                                      pq=pb[6 + nh]:
                                      e.activation(
                                          rsb[tt][:, nh * 512:(nh + 1) * 512],
                                          pq[:, :], AF.Identity,
                                          scale=gates[:, glt * 8:glt * 8 + 1]))
                    st3[("rsePD",)] = v_rse
                    st3[("rse_last",)] = v_rse
                    P.wait("gpsimd", "A", v_rse)
                    P.wait("gpsimd", "rst", P.val("rst"))
                    tok = hf * 1024 + q * 256 + tt * 128
                    P.inc("gpsimd", "rst", lambda e, c=c, tok=tok, tt=tt:
                          e.dma_start(out=rs_i[c][tok:tok + 128, :],
                                      in_=rsb[tt][:, :]), 16)
                    st3[("rstv", tt)] = P.val("rst")
                return _s

            def mk_rs(c):
                def _s(c=c):
                    P.wait("gpsimd", "rst", P.val("rst"))
                    st3[("rs", c)] = P.inc(
                        "gpsimd", "CCS", lambda e, c=c: e.collective_compute(
                            "ReduceScatter", ALU.add,
                            replica_groups=[list(range(NCORES))],
                            ins=[rs_i[c].opt()], outs=[rs_o[c].opt()]))
                return _s

            def push_p3a(c):
                A = p3q.append
                A((1, mk_y1t(c)))
                for kt in range(16):
                    A((4, mk_opj(c, kt)))
                for lt in range(2):
                    A((1, mk_ln1(c, lt)))
                for lt in range(2):
                    A((8, mk_tr(c, lt)))
                A((1, mk_ship(c)))

            def push_p3b(c, hf):
                A = p3q.append
                gc = c if hf == 0 else c + 1
                for q in range(4):
                    A((2, mk_xflq(c, hf, q)))
                for lt in range(8):
                    A((9, mk_rtr(c, hf, lt, gc)))
                A((2, mk_gates(c, hf)))
                for q in range(4):
                    for ft in range(16):
                        A((8, mk_mm1(c, hf, q, ft, gc)))
                    for tt in range(2):
                        A((34, mk_mm2(c, hf, q, tt, gc)))
                    if hf == 0 and q == 1 and c >= 1:
                        for lt in range(2):
                            A((1, mk_ln2(c - 1, c, lt)))
                if hf == 1:
                    A((1, mk_rs(c)))

            v_u_sp = 0
            v_srep = 0
            for ch in range(NCH2):
                lo = ch * LC2
                P.wait("sync", "CCR", v_ar[ch])
                if ch >= 1:
                    P.wait("sync", "ys0", v_y1g[(ch - 1, 0)])
                    P.wait("sync", "ys1", v_y1g[(ch - 1, 1)])
                    P.wait("sync", "pre", v_u_sp)
                for pt in range(2):
                    P.inc("sync", "pre", lambda e, pt=pt, lo=lo: e.dma_start(
                        out=xs2[pt][:, :],
                        in_=xs_d[pt * 128:(pt + 1) * 128, lo:lo + LC2]), 16)
                    P.inc("sync", "pre", lambda e, pt=pt, lo=lo: e.dma_start(
                        out=zs2[pt][:, :],
                        in_=zs_d[pt * 128:(pt + 1) * 128, lo:lo + LC2]), 16)
                P.wait("gpsimd", "CCR", v_ar[ch])
                if ch >= 1:
                    P.wait("gpsimd", "V", v_hc[(ch - 1, NT - 1)])
                    P.wait("gpsimd", "bmc", vg1[ch - 1])
                    P.wait("gpsimd", "smc", v_srep)
                P.inc("gpsimd", "smc", lambda e, ch=ch: e.dma_start(
                    out=h2[:, :],
                    in_=br_o[ch][32:33, :]
                    .unsqueeze(0).broadcast_to([128, 1, LC2])), 16)
                v_srep = P.val("smc")
                P.inc("gpsimd", "bmc", lambda e, ch=ch: e.dma_start(
                    out=bm_rep[:, :],
                    in_=br_o[ch][0:16, :]
                    .unsqueeze(1).broadcast_to([16, 8, LC2])), 16)
                vg1[ch] = P.inc("gpsimd", "bmc", lambda e, ch=ch: e.dma_start(
                    out=cc_rep[:, :],
                    in_=br_o[ch][16:32, :]
                    .unsqueeze(1).broadcast_to([16, 8, LC2])), 16)
                v_pre = P.val("pre")

                P.wait("scalar", "pre", v_pre)
                P.wait("scalar", "smc", v_srep)
                if ch >= 1:
                    P.wait("scalar", "V", v_V_y1[(ch - 1, 1)])
                P.run("scalar", lambda e: e.activation(
                    dt2[0][:, :], h2[:, :], AF.Exp,
                    bias=c_dtbs[:, 0:1], scale=c_dtws[:, 0:1]))
                vAx = P.inc("scalar", "A", lambda e: e.activation(
                    dt2[1][:, :], h2[:, :], AF.Exp,
                    bias=c_dtbs[:, 1:2], scale=c_dtws[:, 1:2]))
                P.wait("scalar", "A", vAx)
                P.run("scalar", lambda e: e.activation(
                    dt2[0][:, :], dt2[0][:, :], AF.Ln, bias=1.0))
                v_A_dt = P.inc("scalar", "A", lambda e: e.activation(
                    dt2[1][:, :], dt2[1][:, :], AF.Ln, bias=1.0))
                P.wait("vector", "A", v_A_dt)
                for pt in range(2):
                    if ch >= 1:
                        P.wait("vector", "T", v_T_tile[(ch - 1, NT - 2 + pt)])
                    P.run("vector", lambda e, pt=pt: e.tensor_tensor(
                        dbx[pt][:, :], dt2[pt][:, :], xs2[pt][:, :], ALU.mult))
                v_V_u = P.inc("vector", "V", lambda e: e.nop())
                P.wait("sync", "V", v_V_u)
                P.wait("sync", "pre", v_pre)
                for pt in range(2):
                    P.inc("sync", "pre", lambda e, pt=pt, lo=lo: e.dma_start(
                        out=u_d[pt * 128:(pt + 1) * 128, lo:lo + LC2],
                        in_=dbx[pt][:, :]), 16)
                    P.inc("sync", "pre", lambda e, pt=pt, lo=lo: e.dma_start(
                        out=dt_d[pt * 128:(pt + 1) * 128, lo:lo + LC2],
                        in_=dt2[pt][:, :]), 16)
                v_u_sp = P.val("pre")

                for t in range(NT):
                    tb = t % 2
                    g = t // 16
                    m = t % 16
                    pk = (ch, t - 2) if t >= 2 else (ch - 1, NT - 2 + t)
                    if pk in v_dbx:
                        P.wait("sync", "V", v_dbx[pk])
                        P.wait("sync", "A", v_A_tile[pk])
                        P.wait("sync", f"ur{tb}", v_sld_rep[pk])
                    if t == 0:
                        P.wait("sync", "pre", v_u_sp)
                    P.inc("sync", f"ur{tb}", lambda e, t=t, lo=lo, tb=tb:
                          e.dma_start(
                              out=u_rep[tb][:, :],
                              in_=u_d[8 * t:8 * t + 8, lo:lo + LC2]
                              .unsqueeze(0).broadcast_to([16, 8, LC2])), 16)
                    v_sld_rep[(ch, t)] = P.inc(
                        "sync", f"ur{tb}", lambda e, t=t, lo=lo, tb=tb:
                        e.dma_start(
                            out=w_rep[tb][:, :],
                            in_=dt_d[8 * t:8 * t + 8, lo:lo + LC2]
                            .unsqueeze(0).broadcast_to([16, 8, LC2])), 16)
                    if pk in v_scan:
                        P.wait("scalar", "V", v_scan[pk])
                    P.wait("scalar", f"ur{tb}", v_sld_rep[(ch, t)])
                    v_A_tile[(ch, t)] = P.inc(
                        "scalar", "A", lambda e, t=t, tb=tb: e.activation(
                            dA2[tb][:, :], w_rep[tb][:, :], AF.Exp,
                            scale=c_A[:, t:t + 1]))
                    P.wait("vector", f"ur{tb}", v_sld_rep[(ch, t)])
                    if t == 0:
                        P.wait("vector", "bmc", vg1[ch])
                    if t <= 1:
                        P.wait("vector", "pre", v_u_sp)
                    if (ch, t) not in ((0, 0), (0, 1)):
                        P.wait("vector", "T", v_T_tile[pk])
                    v_dbx[(ch, t)] = P.inc(
                        "vector", "V", lambda e, tb=tb: e.tensor_tensor(
                            dbx[tb][:, :], u_rep[tb][:, :], bm_rep[:, :],
                            ALU.mult))
                    P.wait("vector", "A", v_A_tile[(ch, t)])
                    if ch == 0:
                        P.run("vector", lambda e, tb=tb: e.tensor_tensor_scan(
                            h2[:, :], dA2[tb][:, :], dbx[tb][:, :], 0.0,
                            ALU.mult, ALU.add))
                    else:
                        P.run("vector", lambda e, tb=tb, t=t: e.tensor_tensor_scan(
                            h2[:, :], dA2[tb][:, :], dbx[tb][:, :],
                            carry[:, t:t + 1], ALU.mult, ALU.add))
                    v_scan[(ch, t)] = P.inc(
                        "vector", "V", lambda e, t=t: e.tensor_copy(
                            carry[:, t:t + 1], h2[:, LC2 - 1:LC2]))
                    v_hc[(ch, t)] = P.inc(
                        "vector", "V", lambda e, tb=tb: e.tensor_tensor(
                            dbx[tb][:, :], h2[:, :], cc_rep[:, :], ALU.mult))
                    pop_steps()
                    P.q_drain(1 << 30)
                    P.wait("tensor", "V", v_hc[(ch, t)])
                    if m == 0 and (ch, g) != (0, 0):
                        pch, pg = (ch, 0) if g == 1 else (ch - 1, 1)
                        P.wait("tensor", "V", v_V_y1[(pch, pg)])
                    for q in range(4):
                        vq = P.inc("tensor", "T",
                                   lambda e, m=m, tb=tb, q=q: e.matmul(
                                       pb[q][:, :],
                                       c_g[:, m * 128:(m + 1) * 128],
                                       dbx[tb][:, q * 512:(q + 1) * 512],
                                       start=(m == 0), stop=(m == 15)))
                    v_T_tile[(ch, t)] = vq
                    if m == 15:
                        P.wait("vector", "T", v_T_tile[(ch, t)])
                        P.wait("vector", "pre", v_u_sp)
                        if ch >= 1:
                            P.wait("vector", f"ys{g}", v_y1g[(ch - 1, g)])
                        for q in range(4):
                            P.run("vector",
                                  lambda e, g=g, q=q:
                                  e.scalar_tensor_tensor(
                                      dt2[g][:, q * 512:(q + 1) * 512],
                                      xs2[g][:, q * 512:(q + 1) * 512],
                                      c_D[:, g:g + 1], pb[q][:, :],
                                      ALU.mult, ALU.add))
                        v_V_y1[(ch, g)] = P.inc(
                            "vector", "V", lambda e, g=g: e.tensor_tensor(
                                xs2[g][:, :], dt2[g][:, :], zs2[g][:, :],
                                ALU.mult))
                        P.wait("sync", "V", v_V_y1[(ch, g)])
                        v_y1g[(ch, g)] = P.inc(
                            "sync", f"ys{g}",
                            lambda e, g=g, ch=ch: e.dma_start(
                                out=a2_i[ch][:, g * 128:(g + 1) * 128, :]
                                .rearrange("j p l -> p j l"),
                                in_=xs2[g][:, :]
                                .rearrange("p (j l) -> p j l", j=8)), 16)

                P.wait("gpsimd", "ys0", v_y1g[(ch, 0)])
                P.wait("gpsimd", "ys1", v_y1g[(ch, 1)])

                def _a2a(e, ch=ch):
                    with allow_ici_all_to_all():
                        return e.collective_compute(
                            "AllToAll", ALU.bypass,
                            replica_groups=[list(range(NCORES))],
                            ins=[a2_i[ch].opt()], outs=[a2_o[ch].opt()])
                st3[("a2a", ch)] = P.inc("gpsimd", "CCX", _a2a)
                if ch >= 1:
                    push_p3b(ch - 1, 0)
                push_p3a(ch)
                if ch >= 1:
                    push_p3b(ch - 1, 1)

            push_p3b(NCH2 - 1, 0)
            push_p3b(NCH2 - 1, 1)
            while p3q:
                fcnt, fn = p3q.popleft()
                fn()
            for lt in range(2):
                mk_ln2(NCH2 - 1, NCH2 - 1, lt)()
            P.q_drain(1 << 30)

            if "u" in taps:
                P.wait("sync", "pre", P.val("pre"))
                P.inc("sync", "tp", lambda e: e.dma_start(
                    out=tap_outs["u"], in_=u_d), 16)
            if "x1" in taps:
                P.wait("sync", "x1s0", P.val("x1s0"))
                P.wait("sync", "x1s1", P.val("x1s1"))
                P.inc("sync", "tp", lambda e: e.dma_start(
                    out=tap_outs["x1"], in_=x1_d), 16)

        P.wait("sync", "os0", P.val("os0"))
        P.wait("sync", "os1", P.val("os1"))
        P.wait("sync", "tp", P.val("tp"))

        with nc.Block() as block:
            for eng in _Plan.ENGINES:
                def _mk(eng):
                    def _run(e):
                        for f in P.ops[eng]:
                            f(e)
                    return _run
                getattr(block, eng)(_mk(eng))

    return nc


def _prep_inputs(inputs):
    x = np.asarray(inputs["L0_attn_x"])
    ipw = np.asarray(inputs["L0_attn_in_proj_w"])
    cw = np.asarray(inputs["L0_attn_conv_w"])
    cb = np.asarray(inputs["L0_attn_conv_b"])
    xpw = np.asarray(inputs["L0_attn_x_proj_w"])
    dtw = np.asarray(inputs["L0_attn_dt_w"])
    dtb = np.asarray(inputs["L0_attn_dt_b"])
    alog = np.asarray(inputs["L0_attn_A_log"])
    Dp = np.asarray(inputs["L0_attn_D"])
    wout = np.asarray(inputs["L0_attn_out_proj_w"])
    Wr = np.asarray(inputs["L0_ffn_Wr"])
    br = np.asarray(inputs["L0_ffn_br"])
    W1 = np.asarray(inputs["L0_ffn_W1"])
    b1 = np.asarray(inputs["L0_ffn_b1"])
    W2 = np.asarray(inputs["L0_ffn_W2"])
    b2 = np.asarray(inputs["L0_ffn_b2"])

    xT = np.ascontiguousarray(x[0].T).astype(np.float32)
    x4 = x[0].reshape(NCH2, NCORES, SL, DM)
    p = np.arange(128)
    g_exp = np.zeros((128, 16, 128), np.float32)
    m_idx = 8 * np.arange(16)[None, :] + (p % 8)[:, None]
    g_exp[p[:, None], np.arange(16)[None, :], m_idx] = 1.0
    g_exp = np.ascontiguousarray(g_exp.reshape(128, 16 * 128))
    ident = np.eye(128, dtype=np.float32)
    ones = np.ones((1, 128), np.float32)

    maps = []
    for c in range(NCORES):
        d0 = c * DC
        dd = d0 + 8 * np.arange(NT)[None, :] + (p % 8)[:, None]
        xpw_p = np.zeros((128, 66), np.float32)
        xpw_p[:, 0:33] = xpw[d0 + p, :]
        xpw_p[:, 33:66] = xpw[d0 + 128 + p, :]
        convw_p = np.zeros((128, 8), np.float32)
        convb_p = np.zeros((128, 2), np.float32)
        d_p = np.zeros((128, 2), np.float32)
        dtw_s = np.zeros((128, 2), np.float32)
        dtb_s = np.zeros((128, 2), np.float32)
        for pt in range(2):
            rows = d0 + pt * 128 + p
            convw_p[:, pt * 4:(pt + 1) * 4] = cw[rows, 0, :]
            convb_p[:, pt] = cb[rows]
            d_p[:, pt] = Dp[rows]
            dtw_s[:, pt] = dtw[rows]
            dtb_s[:, pt] = dtb[rows]
        perm = [c] + [e for e in range(NCORES) if e != c]
        wr_p = np.zeros((128, 64), np.float32)
        for kt in range(8):
            wr_p[:, kt * 8:(kt + 1) * 8] = Wr[kt * 128:(kt + 1) * 128, perm]
        b1_p = np.zeros((128, 16), np.float32)
        for j in range(16):
            b1_p[:, j] = b1[c, j * 128 + p]
        maps.append({
            "xT": xT,
            "x_res": np.ascontiguousarray(x4[:, c].reshape(TC, DM)),
            "winc": np.ascontiguousarray(np.concatenate(
                [ipw[:, d0:d0 + DC], ipw[:, DI + d0:DI + d0 + DC]], axis=1)),
            "xpw_pat": xpw_p,
            "dtw_sm": dtw_s,
            "dtb_sm": dtb_s,
            "alog_pat": np.ascontiguousarray(
                alog[dd, (p // 8)[:, None]]).astype(np.float32),
            "d_pat": d_p,
            "convw_pat": convw_p,
            "convb_pat": convb_p,
            "g_exp": g_exp,
            "ident_in": ident,
            "ones_in": ones,
            "wout_in": wout,
            "wr_pat": wr_p,
            "br_in": br[perm].reshape(1, 8).astype(np.float32),
            "b1_my": b1_p,
            "b2_my": b2[c].reshape(1, DM).astype(np.float32),
            "w1c": np.ascontiguousarray(W1[c]),
            "w2c": np.ascontiguousarray(W2[c]),
        })
    return maps


def kernel(**inputs):
    maps = _prep_inputs(inputs)
    nc = build_program()
    res = run_bass_kernel_spmd(nc, maps, core_ids=list(range(NCORES)))
    stack = np.stack([res.results[c]["out"] for c in range(NCORES)])
    full = stack.reshape(NCORES, NCH2, SL, DM).transpose(1, 0, 2, 3)
    return np.ascontiguousarray(full.reshape(1, L, DM)).astype(np.float32)
